# version 33
# baseline (speedup 1.0000x reference)
import math
from contextlib import ExitStack

import numpy as np
import ml_dtypes

import concourse.bass as bass
import concourse.mybir as mybir
from concourse.bass_utils import run_bass_kernel_spmd

BF16 = mybir.dt.bfloat16
F32 = mybir.dt.float32
ALU = mybir.AluOpType
ACT = mybir.ActivationFunctionType
AX = mybir.AxisListType.X

B_, S_, D_ = 2, 4096, 4096
H_, QL_, KVL_, DN_, DR_, DV_ = 32, 1536, 512, 128, 64, 128
NE_, RK_ = 64, 16
NTOK = 1024
SCALE = 1.0 / math.sqrt(float(DN_ + DR_))
NEG_BIG = -1.0e9
EPS = 1e-5

bf16 = ml_dtypes.bfloat16


class KB:
    ENGS = ("tensor", "vector", "scalar", "gpsimd", "sync")

    def __init__(self, sems):
        self.ops = {e: [] for e in self.ENGS}
        self.sems = sems
        self.cnt = {k: 0 for k in sems}
        self.bankrel = {}

    def raw(self, eng, fn):
        self.ops[eng].append(fn)
        return None

    def op(self, eng, fn, inc=None):
        if inc is None:
            self.ops[eng].append(fn)
            return None
        nm, amt = inc
        self.cnt[nm] += amt
        h = self.sems[nm]

        def wrapped(e, fn=fn, h=h, amt=amt):
            fn(e).then_inc(h, amt)
        self.ops[eng].append(wrapped)
        return self.cnt[nm]

    def wait(self, eng, nm, val):
        if val is None or val <= 0:
            return
        h = self.sems[nm]
        self.ops[eng].append(lambda e, h=h, v=val: e.wait_ge(h, v))

    def drain(self, eng):
        self.ops[eng].append(lambda e: e.drain())

    def t(self, fn):
        return self.op("tensor", fn, inc=("T", 1))

    def v(self, fn):
        return self.op("vector", fn, inc=("V", 1))

    def s(self, fn):
        return self.op("scalar", fn, inc=("S", 1))

    def guard(self, *banks):
        for bk in banks:
            rel = self.bankrel.get(bk)
            if rel:
                self.wait("tensor", rel[0], rel[1])

    def release(self, bk, sem_name, milestone):
        self.bankrel[bk] = (sem_name, milestone)



def CP(e, d, s):
    if hasattr(e, "tensor_copy"):
        return e.tensor_copy(d, s)
    return e.copy(d, s)


def SMUL(e, d, s, c):
    if hasattr(e, "tensor_scalar"):
        return e.tensor_scalar(d, s, c, None, ALU.mult)
    return e.mul(d, s, c)


def view3(t_ap, inner):
    n = t_ap.shape[1] // inner
    return t_ap.rearrange("p (c j) -> p c j", c=n)


def build_nc():
    nc = bass.Bass("TRN2", target_bir_lowering=False)

    def din(name, shape, dtype=F32):
        return nc.dram_tensor(name, list(shape), dtype, kind="ExternalInput").ap()

    XKV = din("xkv", (S_, D_))
    XOWN = din("xown", (NTOK, D_))
    WQD = din("wqd", (12, 128, 4096))
    WQN = din("wqn", (32, 128, 1536))
    WQR = din("wqr", (2, 128, 12288))
    WKVC = din("wkvc", (128, 32, 576))
    WUK = din("wuk", (D_, KVL_))
    WUVT = din("wuvT", (H_, 128, 512))
    WOUT = din("wout", (16, 128, 8192))
    A2 = din("a2", (8, 128, 4096))
    B2 = din("b2", (16, 128, 2048))
    WRT = din("wrt", (128, 2048))
    C_IDB = din("c_idb", (128, 128), BF16)
    C_IDF = din("c_idf", (128, 128), F32)
    C_MU = din("c_mu", (128, 128), BF16)
    C_MV = din("c_mv", (128, 512), BF16)
    COSQ = din("cosq", (NTOK, DR_))
    SINQ = din("sinq", (NTOK, DR_))
    COSK = din("cosk", (S_, DR_))
    SINK = din("sink", (S_, DR_))

    QNS = nc.dram_tensor("qnope_scr", [128, H_, NTOK], BF16, kind="Internal").ap()
    QRS = nc.dram_tensor("qrope_scr", [128, 16, NTOK], BF16, kind="Internal").ap()
    HGS = nc.dram_tensor("hg_scr", [128, 8, NTOK], BF16, kind="Internal").ap()
    KTS = nc.dram_tensor("ckvtm_scr", [128, S_ // 128, KVL_], BF16, kind="Internal").ap()
    ATS = nc.dram_tensor("attn_scr", [128, H_, NTOK], BF16, kind="Internal").ap()
    WOUTC = nc.dram_tensor("wout_c", [16, 128, 8192], BF16, kind="Internal").ap()
    WKVCC = nc.dram_tensor("wkvc_c", [128, 18432], BF16, kind="Internal").ap()
    B2C = nc.dram_tensor("b2_c", [16, 128, 2048], BF16, kind="Internal").ap()
    OUT = nc.dram_tensor("out_own", [NTOK, D_], F32, kind="ExternalOutput").ap()

    es = ExitStack()
    with es:
        sem_names = ["T", "V", "S", "P", "yx0", "yx1", "ya", "ysb", "ysa",
                     "yq0", "yq1", "yat0", "yat1", "yt0", "yt1",
                     "g0", "g1", "gk0", "gk1", "gk2", "gk3", "gw0", "gw1",
                     "yd0", "yd1", "yd2", "yd3", "yo0", "yo1", "gc", "gq", "yx2",
                     "yc0", "yc1", "yc2", "yc3", "yc4", "yc5", "yc6", "yc7"]
        sems = {nm: es.enter_context(nc.semaphore(nm)) for nm in sem_names}
        kb = KB(sems)

        def sbuf(name, shape, dtype):
            return es.enter_context(nc.sbuf_tensor(name, list(shape), dtype))

        idb = sbuf("idb", (128, 128), BF16)
        idf = sbuf("idf", (128, 128), F32)
        mU = sbuf("mU", (128, 128), BF16)
        mV = sbuf("mV", (128, 512), BF16)
        xn_tm = [sbuf("xn_tm0", (128, 4096), BF16), sbuf("xn_tm1", (128, 4096), BF16)]
        sqd = sbuf("sqd", (128, 2048), BF16)
        stats = sbuf("stats", (128, 64), F32)
        top16 = sbuf("top16", (128, 16), F32)
        top16e = sbuf("top16e", (128, 16), F32)

        ST = {}
        for i, nm in enumerate(["xsum", "xsqa", "xsqb", "negm", "m2", "ex2", "varv",
                                "lnv", "rstd", "nmr", "negm16", "l16", "linv16",
                                "scl2", "negmg", "negmA", "negmB", "fdA", "fdB",
                                "fA", "fB", "lgA", "lgB", "lacc", "linv",
                                "lacc1", "linv1", "fC", "fD", "lgC", "lgD", "zero",
                                "negmC", "negmD", "fdC", "fdD",
                                "lacc2", "lacc3", "linv2", "linv3", "xsum2",
                                "xsumZ", "xsum2Z", "xsqaZ", "xsqbZ", "negmZ",
                                "m2Z", "ex2Z", "varvZ", "lnvZ", "rstdZ", "nmrZ"]):
            ST[nm] = stats[:, i:i + 1]

        P1B, P2B, PFB = 96256, 40960, 39680
        pool1 = sbuf("pool1", (128, P1B), mybir.dt.uint8)
        pool2 = sbuf("pool2", (128, P2B), mybir.dt.uint8)
        poolf = sbuf("poolf", (128, PFB), mybir.dt.uint8)

        def pview(pool, name, shape, dtype, off):
            nbytes = int(np.prod(shape[1:])) * mybir.dt.size(dtype)
            assert off % 32 == 0, name
            assert off + nbytes <= pool.shape[1], (name, off, nbytes)
            base = nc.lookup_mloc(pool).addr
            return nc.alloc_sbuf_tensor_at(name, list(shape), dtype,
                                           offset=base + off)

        xnfmB = pview(pool1, "xnfmB", (128, 32768), BF16, 0)
        cqfm = pview(pool1, "cqfm", (128, 12288), BF16, 65536)
        qn_stg = [pview(pool1, "qn_stg0", (128, 1024), BF16, 90112),
                  pview(pool1, "qn_stg1", (128, 1024), BF16, 92160)]
        tr_stg = [pview(pool1, "tr_stg0", (128, 512), BF16, 94208),
                  pview(pool1, "tr_stg1", (128, 512), BF16, 95232)]
        wqrb = pview(pool1, "wqrb", (128, 24576), BF16, 0)
        qro_tm = [pview(pool1, "qro0", (128, 2048), BF16, 49152),
                  pview(pool1, "qro1", (128, 2048), BF16, 53248)]
        slabA2 = [pview(pool1, "slabA0", (128, 8192), BF16, 0),
                  pview(pool1, "slabA1", (128, 8192), BF16, 16384)]
        wkvcb = pview(pool1, "wkvcb", (128, 18432), BF16, 32768)
        ktm_stg = [pview(pool1, "ktm0", (128, 2048), BF16, 69632),
                   pview(pool1, "ktm1", (128, 2048), BF16, 73728)]
        krtm2 = [pview(pool1, "krtm", (128, 256), BF16, 77824),
                 pview(pool1, "krtmB", (128, 256), BF16, 78336)]
        ckvtm = pview(pool1, "ckvtm", (128, 16384), BF16, 0)
        qropefm = pview(pool1, "qropefm", (128, 16384), BF16, 32768)
        qlat = pview(pool1, "qlat", (128, 4096), BF16, 65536)
        probsb = [pview(pool1, "probsb0", (128, 1024), BF16, 73728),
                  pview(pool1, "probsb1", (128, 1024), BF16, 75776),
                  None, None]
        probsT = [pview(pool1, "probsT0", (128, 1024), BF16, 77824),
                  pview(pool1, "probsT1", (128, 1024), BF16, 79872),
                  None, None]
        olatb = pview(pool1, "olatb", (128, 512), BF16, 81920)
        olatT = pview(pool1, "olatT", (128, 512), BF16, 82944)
        attnstg = [pview(pool1, "attnstg0", (128, 1024), BF16, 83968),
                   pview(pool1, "attnstg1", (128, 1024), BF16, 86016)]
        qnopeb = [pview(pool1, "qnopeb0", (128, 1024), BF16, 88064),
                  pview(pool1, "qnopeb1", (128, 1024), BF16, 90112)]
        wukb = [pview(pool1, "wukb0", (128, 512), BF16, 92160),
                pview(pool1, "wukb1", (128, 512), BF16, 93184)]
        wuvb = [pview(pool1, "wuvb0", (128, 512), BF16, 94208),
                pview(pool1, "wuvb1", (128, 512), BF16, 95232)]
        attn_cat = pview(pool1, "attn_cat", (128, 32768), BF16, 0)
        b2b = [pview(pool1, "b2b0", (128, 2048), BF16, 65536),
               pview(pool1, "b2b1", (128, 2048), BF16, 69632)]
        hgD = pview(pool1, "hgD", (128, 8192), BF16, 73728)

        wbuf = [pview(pool2, "wbuf0", (128, 4096), BF16, 0),
                pview(pool2, "wbuf1", (128, 4096), BF16, 8192)]
        ckvfm = pview(pool2, "ckvfm", (128, 16384), BF16, 0)
        krdup = pview(pool2, "krdup", (128, 4096), BF16, 32768)
        woutb = [pview(pool2, "woutb0", (128, 8192), BF16, 0),
                 pview(pool2, "woutb1", (128, 8192), BF16, 16384)]

        xbuf = [pview(poolf, "xbuf0", (128, 4096), F32, 0),
                pview(poolf, "xbuf1", (128, 4096), F32, 16384),
                pview(pool2, "xbuf2", (128, 4096), F32, 16384)]
        ctab = [pview(poolf, "ctab0", (128, 64), F32, 16384),
                pview(poolf, "ctab1", (128, 64), F32, 16640)]
        stab = [pview(poolf, "stab0", (128, 64), F32, 16896),
                pview(poolf, "stab1", (128, 64), F32, 17152)]
        tba = pview(poolf, "tba", (128, 512), F32, 17408)
        tbb = pview(poolf, "tbb", (128, 512), F32, 19456)
        routf = pview(poolf, "routf", (64, 1024), F32, 21504)
        rtm = pview(poolf, "rtm", (128, 64), F32, 25600)
        gtmp = pview(poolf, "gtmp", (128, 64), F32, 25856)
        efb = pview(poolf, "efb", (128, 64), F32, 26112)
        gdup = pview(poolf, "gdup", (128, 1024), F32, 26368)
        ktabc = [pview(poolf, "ktabc0", (128, 256), F32, 32768),
                 pview(poolf, "ktabc1", (128, 256), F32, 33792)]
        ktabs = [pview(poolf, "ktabs0", (128, 256), F32, 34816),
                 pview(poolf, "ktabs1", (128, 256), F32, 35840)]
        krraw = pview(poolf, "krraw", (64, 256), F32, 36864)
        krf32 = pview(poolf, "krf32", (128, 128), F32, 37888)
        krtA = pview(poolf, "krtA", (128, 64), F32, 38400)
        krtB = pview(poolf, "krtB", (128, 64), F32, 38656)
        accb2 = [pview(poolf, "accb", (128, 512), F32, 0),
                 pview(poolf, "accb1", (128, 512), F32, 2048)]
        olatb2 = [None, None]
        olatT2 = [None, None]
        olatb2[1] = pview(poolf, "olatb1", (128, 512), BF16, 4096)
        olatT2[1] = pview(poolf, "olatT1", (128, 512), BF16, 5120)
        probsT_extra = [pview(poolf, "probsT2", (128, 1024), BF16, 6144),
                        pview(poolf, "probsT3", (128, 1024), BF16, 8192)]
        accb_x = [pview(poolf, "accb2b", (128, 512), F32, 10240),
                  pview(poolf, "accb3b", (128, 512), F32, 12288)]
        olatb_x = [pview(poolf, "olatbX", (128, 512), BF16, 14336),
                   pview(poolf, "olatbY", (128, 512), BF16, 15360)]
        olatT_x = [pview(poolf, "olatTX", (128, 512), BF16, 16384),
                   pview(poolf, "olatTY", (128, 512), BF16, 17408)]
        qlat1 = pview(poolf, "qlat1", (128, 8192), BF16, 18432)
        probsb_x = [pview(poolf, "probsb2", (128, 1024), BF16, 30720),
                    pview(poolf, "probsb3", (128, 1024), BF16, 32768)]
        wukb_x = [pview(poolf, "wukb2", (128, 512), BF16, 26624),
                  pview(poolf, "wukb3", (128, 512), BF16, 27648)]
        wuvb_x = [pview(poolf, "wuvb2", (128, 512), BF16, 28672),
                  pview(poolf, "wuvb3", (128, 512), BF16, 29696)]
        dsb = [pview(poolf, "dsb0", (128, 512), F32, 0),
               pview(poolf, "dsb1", (128, 512), F32, 2048),
               pview(poolf, "dsb2", (128, 512), F32, 12288),
               pview(poolf, "dsb3", (128, 512), F32, 14336)]
        xdb = [pview(poolf, "xdb0", (128, 512), F32, 4096),
               pview(poolf, "xdb1", (128, 512), F32, 6144),
               pview(poolf, "xdb2", (128, 512), F32, 16384),
               pview(poolf, "xdb3", (128, 512), F32, 18432)]
        outbf = [pview(poolf, "outb0", (128, 512), F32, 8192),
                 pview(poolf, "outb1", (128, 512), F32, 10240)]

        ps = [es.enter_context(nc.psum_tensor(f"ps{i}", [128, 1024], F32))
              for i in range(4)]

        def bank(i):
            return ps[i // 2][:, (i % 2) * 512:(i % 2) * 512 + 512]

        def bankb(i):
            return ps[i // 2][:].bitcast(BF16)[:, (i % 2) * 1024:(i % 2) * 1024 + 1024]

        def pair(i):
            return ps[i][:]

        def t_run(fns):
            for f in fns[:-1]:
                kb.raw("tensor", f)
            return kb.t(fns[-1])

        uses = {}

        def use(nm):
            uses[nm] = uses.get(nm, 0) + 1
            return uses[nm]

        def dma(eng, sem_name, out_ap, in_ap):
            kb.op(eng, lambda e, o=out_ap, i=in_ap: e.dma_start(out=o, in_=i),
                  inc=(sem_name, 16))
            return 16 * use(sem_name)

        dma("sync", "ya", idb[:], C_IDB[:])
        dma("sync", "ya", idf[:], C_IDF[:])
        dma("sync", "ya", mU[:], C_MU[:])
        consts_done = dma("sync", "ya", mV[:], C_MV[:])
        for eng in KB.ENGS:
            kb.wait(eng, "ya", consts_done)
        kb.v(lambda e: e.memset(ST["zero"], 0.0))
        kb.drain("vector")

        lnfree = {("V", 0): 0, ("V", 1): 0, ("S", 0): 0, ("S", 1): 0,
                  ("V", 2): 0, ("S", 2): 0}

        def R(nm, slot):
            return ST[nm + ("Z" if slot else "")]

        def ln_load(xsrc_row_ap, bslot):
            sx = f"yx{bslot}"
            kb.wait("sync", "V", lnfree[("V", bslot)])
            kb.wait("sync", "S", lnfree[("S", bslot)])
            dma("sync", sx, xbuf[bslot][:, 0:2048], xsrc_row_ap[:, 0:2048])
            return dma("sync", sx, xbuf[bslot][:, 2048:4096],
                       xsrc_row_ap[:, 2048:4096])

        def emit_ln_stats_a(xsrc_row_ap, slot, preloaded=None, bslot=None):
            if bslot is None:
                bslot = slot
            sx = f"yx{bslot}"
            if preloaded is None:
                nl = ln_load(xsrc_row_ap, bslot)
            else:
                nl = preloaded
            xb = xbuf[bslot][:]
            kb.wait("vector", sx, nl)
            kb.drain("vector")
            kb.raw("vector", lambda e, xb=xb, slot=slot:
                   e.reduce_sum(R("xsum", slot), xb[:, 0:2048], AX))
            kb.wait("scalar", sx, nl)
            kb.drain("scalar")
            kb.raw("scalar", lambda e, xb=xb, slot=slot: e.activation(
                sqd[:], xb[:, 0:2048], ACT.Square, accum_out=R("xsqa", slot)))
            kb.drain("scalar")
            kb.raw("scalar", lambda e, xb=xb, slot=slot: e.activation(
                sqd[:], xb[:, 2048:4096], ACT.Square, accum_out=R("xsqb", slot)))
            kb.drain("scalar")
            s_sq = kb.s(lambda e, xb=xb, slot=slot: e.activation(
                sqd[:, 0:2048], xb[:, 2048:4096], ACT.Identity,
                accum_out=R("xsum2", slot)))
            lnfree[("S", bslot)] = s_sq
            lnpend[slot] = (xb, s_sq, bslot)

        def emit_ln_stats_b(slot):
            xb, s_sq, bslot = lnpend[slot]
            kb.wait("vector", "S", s_sq)
            kb.drain("vector")
            kb.raw("vector", lambda e, slot=slot: e.tensor_add(
                R("xsum", slot), R("xsum", slot), R("xsum2", slot)))
            kb.drain("vector")
            kb.raw("vector", lambda e, slot=slot: e.tensor_scalar(
                R("negm", slot), R("xsum", slot), -1.0 / 4096.0, None, ALU.mult))
            kb.raw("vector", lambda e, slot=slot: e.tensor_add(
                R("ex2", slot), R("xsqa", slot), R("xsqb", slot)))
            kb.drain("vector")
            kb.raw("vector", lambda e, slot=slot: e.tensor_mul(
                R("m2", slot), R("negm", slot), R("negm", slot)))
            kb.raw("vector", lambda e, slot=slot: e.tensor_scalar(
                R("ex2", slot), R("ex2", slot), 1.0 / 4096.0, None, ALU.mult))
            kb.drain("vector")
            v_stats = kb.v(lambda e, slot=slot: e.scalar_tensor_tensor(
                R("varv", slot), R("ex2", slot), EPS, R("m2", slot),
                ALU.add, ALU.subtract))
            kb.wait("scalar", "V", v_stats)
            kb.drain("scalar")
            kb.raw("scalar", lambda e, slot=slot: e.activation(
                R("lnv", slot), R("varv", slot), ACT.Ln))
            kb.drain("scalar")
            s_rstd = kb.s(lambda e, slot=slot: e.activation(
                R("rstd", slot), R("lnv", slot), ACT.Exp, scale=-0.5))
            kb.wait("vector", "S", s_rstd)
            kb.drain("vector")
            kb.raw("vector", lambda e, slot=slot: e.tensor_mul(
                R("nmr", slot), R("negm", slot), R("rstd", slot)))
            kb.drain("vector")
            tguard = lnfree.get(("T", slot), 0)
            kb.wait("vector", "T", tguard)
            v_xn = kb.v(lambda e, xb=xb, slot=slot: e.tensor_scalar(
                xn_tm[slot][:], xb, R("rstd", slot), R("nmr", slot),
                ALU.mult, ALU.add))
            lnfree[("V", bslot)] = v_xn
            return v_xn

        def emit_ln_trans(slot, v_xn, slab3, colbase, tr_banks, slab_guard=0):
            tm = None
            for gi in range(4):
                bk = tr_banks[gi % len(tr_banks)]
                kb.wait("tensor", "V", v_xn)
                kb.guard(bk)
                fns = []
                for k in range(8):
                    dc = gi * 8 + k
                    fns.append(lambda e, o=bankb(bk)[:, k * 128:(k + 1) * 128],
                               i=xn_tm[slot][:, dc * 128:(dc + 1) * 128]:
                               e.transpose(o, i, idb[:]))
                tm = t_run(fns)
                eng = "vector" if gi % 2 == 0 else "scalar"
                enm = "V" if eng == "vector" else "S"
                kb.wait(eng, "T", tm)
                kb.wait(eng, "T", slab_guard)
                dst = slab3[:, gi * 8:(gi + 1) * 8, colbase:colbase + 128]
                src = view3(bankb(bk), 128)[:, 0:8, :]
                mv = kb.op(eng, lambda e, d=dst, s=src: CP(e, d, s),
                           inc=(enm, 1))
                kb.release(bk, enm, mv)
            lnfree[("T", slot)] = tm

        lnpend = {}

        def stage_barrier():
            snap = dict(kb.cnt)
            for eng in KB.ENGS:
                for nm, v in snap.items():
                    if nm in ("yx0", "yx1", "yx2"):
                        continue
                    kb.wait(eng, nm, v)

        xnfmB3 = view3(xnfmB[:], 1024)
        cq3 = view3(cqfm[:], 1024)
        lnevac = {}
        nlq = {0: ln_load(XOWN[0:128, :], 0)}
        for ch in range(8):
            if ch + 1 < 8:
                nlq[ch + 1] = ln_load(XOWN[(ch + 1) * 128:(ch + 2) * 128, :],
                                      (ch + 1) % 3)
            if ch > 0:
                vp = emit_ln_stats_b((ch - 1) % 2)
                emit_ln_trans((ch - 1) % 2, vp, xnfmB3, (ch - 1) * 128,
                              (4, 5, 6, 7))
                lnevac[ch - 1] = (kb.cnt["V"], kb.cnt["S"])
            emit_ln_stats_a(XOWN[ch * 128:(ch + 1) * 128, :], ch % 2,
                            preloaded=nlq[ch], bslot=ch % 3)
        vp = emit_ln_stats_b(1)
        emit_ln_trans(1, vp, xnfmB3, 7 * 128, (4, 5, 6, 7))
        lnevac[7] = (kb.cnt["V"], kb.cnt["S"])

        kb.wait("tensor", "V", kb.cnt["V"])
        kb.wait("tensor", "S", kb.cnt["S"])

        wfree = {0: 0, 1: 0}

        def b_proj(widx, n_out_groups, rhs3, nrhs_chunks, out_pair, lhsT_view_fn,
                   src_fn, evac_fn, tag, after_load=None, oi0=0):
            for oi in range(oi0, oi0 + n_out_groups):
                slot = oi % 2
                g = f"g{slot}"
                kb.wait("gpsimd", "T", wfree[slot])
                dma("gpsimd", g, *src_fn(oi, slot))
                if after_load:
                    after_load(oi)
                pp = oi % 2
                kb.wait("tensor", g, 16 * uses[g])
                kb.guard(2 * pp, 2 * pp + 1)
                fns = []
                for dc in range(nrhs_chunks):
                    for th in range(2):
                        fns.append(lambda e, o=pair(pp)[:, th * 512:(th + 1) * 512],
                                   l=lhsT_view_fn(oi, slot, dc),
                                   r=rhs3[:, dc, th * 512:(th + 1) * 512],
                                   st=(dc == 0), sp=(dc == nrhs_chunks - 1):
                                   e.matmul(o, l, r, start=st, stop=sp))
                tm = t_run(fns)
                wfree[slot] = tm
                evac_fn(oi, pp, tm)

        def cq_src(oi, slot):
            return (wbuf[slot][:], WQD[oi])

        def cq_lhs(oi, slot, dc):
            return view3(wbuf[slot][:], 128)[:, dc, :]

        def cq_evac(oi, pp, tm):
            eng = "vector" if oi % 2 == 0 else "scalar"
            enm = "V" if eng == "vector" else "S"
            kb.wait(eng, "T", tm)
            mv = kb.op(eng, lambda e, d=cq3[:, oi, :], s=pair(pp):
                       CP(e, d, s), inc=(enm, 1))
            kb.release(2 * pp, enm, mv)
            kb.release(2 * pp + 1, enm, mv)

        for slot in (0, 1):
            dma("gpsimd", f"g{slot}", wbuf[slot][:], WQD[slot])
        kb.guard(0, 1, 2, 3)
        tmg = {}
        for step in range(10):
            for oi in (0, 1):
                ch = step - 2 * oi
                if not (0 <= ch <= 7):
                    continue
                kb.wait("tensor", "V", lnevac[ch][0])
                kb.wait("tensor", "S", lnevac[ch][1])
                if ch == 0:
                    kb.wait("tensor", f"g{oi}", 16 * uses[f"g{oi}"])
                fns = []
                for k in range(4):
                    dc = ch * 4 + k
                    for th in range(2):
                        fns.append(lambda e, o=pair(oi)[:, th * 512:(th + 1) * 512],
                                   l=view3(wbuf[oi][:], 128)[:, dc, :],
                                   r=xnfmB3[:, dc, th * 512:(th + 1) * 512],
                                   st=(dc == 0), sp=(dc == 31):
                                   e.matmul(o, l, r, start=st, stop=sp))
                if ch == 7:
                    tmg[oi] = t_run(fns)
                else:
                    for f in fns:
                        kb.raw("tensor", f)
        cq_evac(0, 0, tmg[0])
        cq_evac(1, 1, tmg[1])
        wfree[0] = tmg[0]
        wfree[1] = tmg[1]
        b_proj(0, 10, xnfmB3, 32, None, cq_lhs, cq_src, cq_evac, "cq", oi0=2)

        slot = 0
        kb.wait("gpsimd", "T", wfree[slot])
        dma("gpsimd", "g0", wbuf[0][:, 0:2048], WRT[:])
        kb.wait("tensor", "g0", 16 * uses["g0"])
        kb.guard(0, 1)
        fns = []
        for dc in range(32):
            for th in range(2):
                fns.append(lambda e, o=pair(0)[0:64, th * 512:(th + 1) * 512],
                           l=view3(wbuf[0][:], 64)[:, dc, :],
                           r=xnfmB3[:, dc, th * 512:(th + 1) * 512],
                           st=(dc == 0), sp=(dc == 31):
                           e.matmul(o, l, r, start=st, stop=sp))
        tm = t_run(fns)
        wfree[0] = tm
        kb.wait("vector", "T", tm)
        v_rout = kb.v(lambda e: e.tensor_copy(routf[:], pair(0)[0:64, :]))
        kb.release(0, "V", v_rout)
        kb.release(1, "V", v_rout)

        v_gprev = v_rout
        for tc in range(8):
            kb.wait("tensor", "V", v_gprev)
            kb.guard(4)
            t_rt = kb.t(lambda e, i=routf[0:64, tc * 128:(tc + 1) * 128]:
                        e.transpose(bank(4)[:, 0:64], i, idf[0:64, 0:64]))
            kb.wait("vector", "T", t_rt)
            v_rtm = kb.v(lambda e: e.tensor_copy(rtm[:], bank(4)[:, 0:64]))
            kb.release(4, "V", v_rtm)
            kb.drain("vector")
            kb.raw("vector", lambda e: e.max(top16[:, 0:8], rtm[:]))
            kb.drain("vector")
            kb.raw("vector", lambda e: e.match_replace(gtmp[:], top16[:, 0:8], rtm[:], -3e38))
            kb.drain("vector")
            kb.raw("vector", lambda e: e.max(top16[:, 8:16], gtmp[:]))
            kb.drain("vector")
            v_t16 = kb.v(lambda e: e.tensor_scalar(
                ST["negm16"], top16[:, 0:1], -1.0, None, ALU.mult))
            kb.wait("scalar", "V", v_t16)
            kb.raw("scalar", lambda e: e.activation(
                top16e[:], top16[:], ACT.Exp, bias=ST["negm16"], accum_out=ST["l16"]))
            kb.drain("scalar")
            s_ef = kb.s(lambda e: e.activation(
                efb[:], rtm[:], ACT.Exp, bias=ST["negm16"]))
            kb.wait("vector", "S", s_ef)
            kb.raw("vector", lambda e: e.reciprocal(ST["linv16"], ST["l16"]))
            kb.raw("vector", lambda e: e.tensor_scalar(
                gtmp[:], rtm[:], top16[:, 15:16], None, ALU.is_ge))
            kb.drain("vector")
            kb.raw("vector", lambda e: e.tensor_scalar(
                ST["scl2"], ST["linv16"], 2.0, None, ALU.mult))
            kb.raw("vector", lambda e: e.tensor_mul(rtm[:], efb[:], gtmp[:]))
            kb.drain("vector")
            v_g = kb.v(lambda e: e.tensor_scalar(
                rtm[:], rtm[:], ST["scl2"], None, ALU.mult))
            kb.wait("tensor", "V", v_g)
            kb.guard(5)
            t_g = kb.t(lambda e: e.transpose(bank(5)[0:64, 0:128], rtm[:], idf[:]))
            kb.wait("vector", "T", t_g)
            v_gprev = kb.v(lambda e, d=gdup[0:64, tc * 128:(tc + 1) * 128]:
                           e.tensor_copy(d, bank(5)[0:64, 0:128]))
            kb.release(5, "V", v_gprev)
        kb.wait("sync", "V", v_gprev)
        gdup_done = dma("sync", "ya", gdup[64:128, :], gdup[0:64, :])

        def h_src(oi, slot):
            return (wbuf[slot][:], A2[oi])

        def h_evac(oi, pp, tm):
            slot = oi % 2
            kb.wait("vector", "T", tm)
            kb.wait("vector", "ya", gdup_done)
            kb.wait("vector", "ysb", 16 * uses.get("ysb", 0))
            v_hg = kb.v(lambda e, d=qn_stg[slot][:], s=pair(pp):
                        e.tensor_tensor(d, s, gdup[:], ALU.mult))
            kb.release(2 * pp, "V", v_hg)
            kb.release(2 * pp + 1, "V", v_hg)
            kb.wait("sync", "V", v_hg)
            dma("sync", "ysb", HGS[:, oi, :], qn_stg[slot][:])

        b_proj(0, 8, xnfmB3, 32, None, cq_lhs, h_src, h_evac, "h")
        h_tm = kb.cnt["T"]

        _gclast = [0]

        def cast(dst, srcw):
            if _gclast[0]:
                kb.wait("gpsimd", "gc", _gclast[0])
            _gclast[0] = dma("gpsimd", "gc", dst, srcw)

        def wqr_hook(oi):
            if oi % 4 == 3 and oi // 4 < 8:
                j = oi // 4
                if j == 0:
                    kb.wait("gpsimd", "T", h_tm)
                dma("gpsimd", "gq",
                    wqrb[:, j * 3072:(j + 1) * 3072],
                    WQR[j // 4][:, (j % 4) * 3072:(j % 4 + 1) * 3072])
                if j == 7:
                    cast(WKVCC[:], WKVC[:].rearrange("p c j -> p (c j)"))

        def qn_src(oi, slot):
            return (wbuf[slot][:, 0:1536], WQN[oi])

        def qn_evac(oi, pp, tm):
            slot = oi % 2
            eng = "vector" if oi % 2 == 0 else "scalar"
            enm = "V" if eng == "vector" else "S"
            kb.wait(eng, "T", tm)
            kb.wait(eng, "ysb", 16 * uses.get("ysb", 0))
            mv = kb.op(eng, lambda e, d=qn_stg[slot][:], s=pair(pp):
                       CP(e, d, s), inc=(enm, 1))
            kb.release(2 * pp, enm, mv)
            kb.release(2 * pp + 1, enm, mv)
            kb.wait("sync", enm, mv)
            dma("sync", "ysb", QNS[:, oi, :], qn_stg[slot][:])

        b_proj(0, 32, cq3, 12, None, cq_lhs, qn_src, qn_evac, "qn",
               after_load=wqr_hook)
        wqr_loaded = 16 * uses["gq"]

        qro_free = {0: 0, 1: 0}
        for tc in range(8):
            tsl = tc % 2
            kb.wait("sync", "V", qro_free[tsl])
            dma("sync", f"yt{tsl}", ctab[tsl][:], COSQ[tc * 128:(tc + 1) * 128, :])
            tab_n = dma("sync", f"yt{tsl}", stab[tsl][:], SINQ[tc * 128:(tc + 1) * 128, :])
            kb.wait("tensor", "gq", wqr_loaded)
            kb.wait("tensor", "V", qro_free[tsl])
            v_drt = {}
            for drt in range(4):
                kb.guard(drt)
                fns = []
                for qc in range(12):
                    fns.append(lambda e, o=bank(drt), drt=drt,
                               l=cq3[:, qc, tc * 128:(tc + 1) * 128],
                               r=view3(wqrb[:], 2048)[:, qc, drt * 512:(drt + 1) * 512],
                               st=(qc == 0), sp=(qc == 11):
                               e.matmul(o, l, r, start=st, stop=sp))
                tm_d = t_run(fns)
                pb3 = view3(bank(drt), 64)
                ct = ctab[tsl][:].unsqueeze(1).broadcast_to([128, 8, 64])
                sl = stab[tsl][:, 0:32].unsqueeze(1).broadcast_to([128, 8, 32])
                sh = stab[tsl][:, 32:64].unsqueeze(1).broadcast_to([128, 8, 32])
                kb.wait("vector", "T", tm_d)
                if drt == 0:
                    kb.wait("vector", f"yt{tsl}", tab_n)
                kb.drain("vector")
                kb.raw("vector", lambda e, pb3=pb3, ct=ct:
                       e.tensor_tensor(view3(tba[:], 64), pb3, ct, ALU.mult))
                kb.raw("vector", lambda e, pb3=pb3, sl=sl:
                       e.tensor_tensor(view3(tbb[:], 64)[:, :, 0:32],
                                       pb3[:, :, 32:64], sl, ALU.mult))
                kb.raw("vector", lambda e, pb3=pb3, sh=sh:
                       e.tensor_tensor(view3(tbb[:], 64)[:, :, 32:64],
                                       pb3[:, :, 0:32], sh, ALU.mult))
                kb.drain("vector")
                v_d = kb.v(lambda e, drt=drt, tsl=tsl:
                           e.tensor_add(qro_tm[tsl][:, drt * 512:(drt + 1) * 512],
                                        tba[:], tbb[:]))
                v_drt[drt] = v_d
                kb.release(drt, "V", v_d)
            v_last = v_drt[3]
            qro_free[tsl] = v_last
            for gi in range(4):
                bk = 4 + gi % 2
                kb.wait("tensor", "V", v_drt[gi])
                kb.guard(bk)
                fns = []
                for k in range(4):
                    fc = gi * 4 + k
                    fns.append(lambda e, o=bankb(bk)[:, k * 128:(k + 1) * 128],
                               i=qro_tm[tsl][:, fc * 128:(fc + 1) * 128]:
                               e.transpose(o, i, idb[:]))
                tmg = t_run(fns)
                sslot = gi % 2
                eng = "vector" if gi % 2 == 0 else "scalar"
                enm = "V" if eng == "vector" else "S"
                kb.wait(eng, "T", tmg)
                kb.wait(eng, "ysb", 16 * uses.get("ysb", 0))
                mv = kb.op(eng, lambda e, d=tr_stg[sslot][:], s=bankb(bk)[:, 0:512]:
                           CP(e, d, s), inc=(enm, 1))
                kb.release(bk, enm, mv)
                kb.wait("sync", enm, mv)
                dst = QRS[:, gi * 4:(gi + 1) * 4, tc * 128:(tc + 1) * 128]
                dma("sync", "ysb", dst, view3(tr_stg[sslot][:], 128))

        b_ysb = 16 * uses.get("ysb", 0)

        kb.wait("sync", "V", lnfree[("V", 0)])
        kb.wait("sync", "S", lnfree[("S", 0)])
        dma("sync", "yx0", xbuf[0][:, 0:2048], XKV[0:128, 0:2048])
        a_pre_nl = dma("sync", "yx0", xbuf[0][:, 2048:4096], XKV[0:128, 2048:4096])
        kb.wait("sync", "V", lnfree[("V", 2)])
        kb.wait("sync", "S", lnfree[("S", 2)])
        dma("sync", "yx2", xbuf[2][:, 0:2048], XKV[128:256, 0:2048])
        a_pre_nl2 = dma("sync", "yx2", xbuf[2][:, 2048:4096], XKV[128:256, 2048:4096])

        stage_barrier()

        slab3P = [view3(slabA2[0][:], 256), view3(slabA2[1][:], 256)]
        ckvfm3 = view3(ckvfm[:], 4096)
        kb.wait("sync", "gc", 16)
        dma("sync", "ya", wkvcb[:, 0:9216], WKVCC[:, 0:9216])
        wkv_loaded = dma("sync", "ya", wkvcb[:, 9216:18432], WKVCC[:, 9216:18432])

        lnfree[("V", 0)] = kb.cnt["V"]
        lnfree[("V", 1)] = kb.cnt["V"]
        lnfree[("S", 0)] = kb.cnt["S"]
        lnfree[("S", 1)] = kb.cnt["S"]
        ktab_free = {0: None, 1: None}
        ktab_ns = {}
        kv_done = {}
        ln_done = {}

        def a_ln_stats_tile(t):
            for i4 in range(2):
                ch = t * 2 + i4
                if ch == 0:
                    emit_ln_stats_a(XKV[0:128, :], 0, preloaded=a_pre_nl)
                elif ch == 1:
                    emit_ln_stats_a(XKV[128:256, :], 1, preloaded=a_pre_nl2,
                                    bslot=2)
                else:
                    emit_ln_stats_a(XKV[ch * 128:(ch + 1) * 128, :], ch % 2)
            return [emit_ln_stats_b((t * 2) % 2), emit_ln_stats_b((t * 2 + 1) % 2)]

        def a_ln_trans_tile(t, vxs):
            slot = t % 2
            for i4 in range(2):
                ch = t * 2 + i4
                emit_ln_trans(ch % 2, vxs[i4], slab3P[slot], i4 * 128, (6, 7),
                              slab_guard=kv_done.get(t - 2, 0))
            ln_done[t] = (kb.cnt["V"], kb.cnt["S"])

        def a_kv_tile(t):
            slot = t % 2
            slab3 = slab3P[slot]
            tsl = t % 2
            kf = ktab_free[tsl]
            if kf:
                kb.wait("sync", kf[0], kf[1])
            dma("sync", f"yt{tsl}", view3(ktabc[tsl][:, 0:128], 64),
                COSK[t * 256:(t + 1) * 256, :].rearrange("(i p) j -> p i j", p=128))
            ktab_ns[t] = dma("sync", f"yt{tsl}", view3(ktabs[tsl][:, 0:128], 64),
                             SINK[t * 256:(t + 1) * 256, :].rearrange("(i p) j -> p i j", p=128))
            kb.wait("tensor", "ya", wkv_loaded)
            lv, ls = ln_done[t]
            kb.wait("tensor", "V", lv)
            kb.wait("tensor", "S", ls)
            kb.guard(0, 1, 2, 3, 4)
            fns = []
            for dc in range(32):
                for rc in range(4):
                    fns.append(lambda e, o=bank(rc)[:, 0:256],
                               l=view3(wkvcb[:], 576)[:, dc, rc * 128:(rc + 1) * 128],
                               r=slab3[:, dc, :], st=(dc == 0), sp=(dc == 31):
                               e.matmul(o, l, r, start=st, stop=sp))
                fns.append(lambda e, o=bank(4)[0:64, 0:256],
                           l=view3(wkvcb[:], 576)[:, dc, 512:576],
                           r=slab3[:, dc, :], st=(dc == 0), sp=(dc == 31):
                           e.matmul(o, l, r, start=st, stop=sp))
            tm = t_run(fns)
            kv_done[t] = tm
            return tm

        post_state = {}
        last_tkb = {}

        def a_kv_post1(t):
            tm = kv_done[t]
            slab3 = slab3P[t % 2]
            tsl = t % 2
            ktab_n = ktab_ns[t]
            kb.wait("vector", "T", tm)
            v_e0 = kb.v(lambda e, t=t: e.tensor_copy(
                ckvfm3[:, 0:2, t * 256:(t + 1) * 256],
                view3(pair(0), 512)[:, :, 0:256]))
            kb.release(0, "V", v_e0)
            kb.release(1, "V", v_e0)
            kb.wait("scalar", "T", tm)
            s_e1 = kb.s(lambda e, t=t: CP(e,
                ckvfm3[:, 2:4, t * 256:(t + 1) * 256],
                view3(pair(1), 512)[:, :, 0:256]))
            kb.release(2, "S", s_e1)
            kb.release(3, "S", s_e1)
            kb.wait("scalar", "T", tm)
            s_kr = kb.s(lambda e: CP(e, krraw[0:64, 0:256], bank(4)[0:64, 0:256]))
            kb.release(4, "S", s_kr)
            kb.wait("tensor", "S", s_kr)
            kb.guard(5)
            fns = []
            for i4 in range(2):
                fns.append(lambda e, o=bank(5)[:, i4 * 64:(i4 + 1) * 64],
                           i=krraw[0:64, i4 * 128:(i4 + 1) * 128]:
                           e.transpose(o, i, idf[0:64, 0:64]))
            t_kt = t_run(fns)
            kb.wait("vector", "T", t_kt)
            v_krf = kb.v(lambda e: e.tensor_copy(krf32[:], bank(5)[:, 0:128]))
            kb.release(5, "V", v_krf)
            kb.wait("vector", f"yt{tsl}", ktab_n)
            kb.wait("vector", "T", last_tkb.get(tsl, 0))
            p_kro = None
            for i4 in range(2):
                pbk = krf32[:, i4 * 64:(i4 + 1) * 64]
                ct = view3(ktabc[tsl][:, 0:128], 64)[:, i4, :]
                slo = view3(ktabs[tsl][:, 0:128], 64)[:, i4, 0:32]
                shi = view3(ktabs[tsl][:, 0:128], 64)[:, i4, 32:64]
                kb.drain("vector")
                kb.raw("vector", lambda e, pbk=pbk, ct=ct:
                       e.tensor_tensor(krtA[:], pbk, ct, ALU.mult))
                kb.raw("vector", lambda e, pbk=pbk, slo=slo:
                       e.tensor_tensor(krtB[:, 0:32], pbk[:, 32:64], slo, ALU.mult))
                kb.raw("vector", lambda e, pbk=pbk, shi=shi:
                       e.tensor_tensor(krtB[:, 32:64], pbk[:, 0:32], shi, ALU.mult))
                kb.drain("vector")
                p_kro = kb.v(lambda e, i4=i4, tsl=tsl: e.tensor_add(
                    krtm2[tsl][:, i4 * 64:(i4 + 1) * 64], krtA[:], krtB[:]))
            ktab_free[tsl] = ("V", p_kro)
            post_state[t] = (p_kro, v_e0, s_e1)

        def a_kv_post2(t):
            p_kro, v_e0, s_e1 = post_state.pop(t)
            tsl = t % 2
            kb.wait("tensor", "V", p_kro)
            kb.guard(5)
            fns = []
            for i4 in range(2):
                fns.append(lambda e, o=bankb(5)[0:64, i4 * 128:(i4 + 1) * 128],
                           i=krtm2[tsl][:, i4 * 64:(i4 + 1) * 64]:
                           e.transpose(o, i, idb[:]))
            t_kb = t_run(fns)
            last_tkb[tsl] = t_kb
            kb.wait("scalar", "T", t_kb)
            s_krf = kb.s(lambda e, t=t: CP(e,
                krdup[0:64, t * 256:(t + 1) * 256], bankb(5)[0:64, 0:256]))
            kb.wait("tensor", "S", s_krf)
            fns = []
            for k in range(8):
                i4, rc = k // 4, k % 4
                fns.append(lambda e, o=bankb(5)[:, k * 128:(k + 1) * 128],
                           i=ckvfm3[:, rc, t * 256 + i4 * 128: t * 256 + (i4 + 1) * 128]:
                           e.transpose(o, i, idb[:]))
            tmg = t_run(fns)
            sslot = t % 2
            kb.wait("vector", "T", tmg)
            kb.wait("vector", "ysa", 16 * uses.get("ysa", 0))
            mv = kb.v(lambda e, d=ktm_stg[sslot][:, 0:1024], s=bankb(5):
                      e.tensor_copy(d, s))
            kb.release(5, "V", mv)
            kb.wait("sync", "V", mv)
            dst = KTS[:, t * 2: t * 2 + 2, :]
            dma("sync", "ysa", dst, view3(ktm_stg[sslot][:, 0:1024], 512))

        pend_vx = {}
        for t in range(18):
            if t < 16:
                pend_vx[t] = a_ln_stats_tile(t)
            if 1 <= t <= 16:
                a_kv_tile(t - 1)
            if 2 <= t <= 17:
                a_kv_post2(t - 2)
            if t < 16:
                a_ln_trans_tile(t, pend_vx.pop(t))
            if 1 <= t <= 16:
                a_kv_post1(t - 1)
        s_all = kb.cnt["S"]
        kb.wait("sync", "S", s_all)
        krdup_done = dma("sync", "ya", krdup[64:128, :], krdup[0:64, :])
        a_ysa = 16 * uses.get("ysa", 0)
        stage_barrier()

        ckvtm3 = view3(ckvtm[:], 512)
        qrfm3 = view3(qropefm[:], 1024)
        qlat3 = view3(qlat[:], 1024)
        kb.wait("sync", "ysa", a_ysa)
        kb.wait("sync", "ysb", b_ysb)
        qr_n = {}
        qr_seen = set()
        for hp in range(2):
            qr_n[hp] = dma("sync", f"yc{hp}", qrfm3[:, hp, :], QRS[:, hp, :])
        c_pre = None
        for ki in range(4):
            c_pre = dma("sync", "ya", ckvtm3[:, ki * 8:(ki + 1) * 8, :],
                        KTS[:, ki * 8:(ki + 1) * 8, :])
        c_pre_pend = [c_pre]
        kb.wait("tensor", "ya", krdup_done)

        olatb2[0] = olatb
        olatT2[0] = olatT
        olatb4 = [olatb2[0], olatb2[1], olatb_x[0], olatb_x[1]]
        olatT4 = [olatT2[0], olatT2[1], olatT_x[0], olatT_x[1]]
        accb4 = [accb2[0], accb2[1], accb_x[0], accb_x[1]]
        probsT[2] = probsT_extra[0]
        probsT[3] = probsT_extra[1]
        probsb[2] = probsb_x[0]
        probsb[3] = probsb_x[1]
        laccP = [ST["lacc"], ST["lacc1"], ST["lacc2"], ST["lacc3"]]
        linvP = [ST["linv"], ST["linv1"], ST["linv2"], ST["linv3"]]
        nm4 = [ST["negmA"], ST["negmB"], ST["negmC"], ST["negmD"]]
        fd4 = [ST["fdA"], ST["fdB"], ST["fdC"], ST["fdD"]]
        fP4 = [ST["fA"], ST["fB"], ST["fC"], ST["fD"]]
        lgP4 = [ST["lgA"], ST["lgB"], ST["lgC"], ST["lgD"]]
        qlatB = [qlat, qlat1]
        qlat3P = [view3(qlat[:], 1024), view3(qlat1[:], 1024)]
        wukb4 = [wukb[0], wukb[1], wukb_x[0], wukb_x[1]]
        wuvb4 = [wuvb[0], wuvb[1], wuvb_x[0], wuvb_x[1]]

        qn_free = {0: 0, 1: 0}
        wk_free = {0: 0, 1: 0}
        sc_free = {0: None, 1: None}
        pv_free = None
        tr_free = None
        oh_free = None
        olat_free = {}
        olT_free = {}
        tr7_free = None
        qlat_free = None

        units_base = []
        for qt in reversed(range(8)):
            d = qt + 1
            ng = (d + 1) // 2
            for g in range(ng):
                units_base.append((qt, g, ng))
        NU1 = len(units_base)

        load_done = {}

        wk_free4 = {0: 0, 1: 0, 2: 0, 3: 0}

        def issue_loads(h):
            hs = h % 2
            h4 = h % 4
            kb.wait("sync", "T", qn_free[hs])
            nq = dma("sync", f"yq{hs}", qnopeb[hs][:], QNS[:, h, :])
            kb.wait("gpsimd", "T", wk_free4[h4])
            dma("gpsimd", f"gk{h4}", wukb4[h4][:], WUK[h * 128:(h + 1) * 128, :])
            nk = dma("gpsimd", f"gk{h4}", wuvb4[h4][:], WUVT[h])
            load_done[h] = (nq, nk)
            if h % 2 == 0 and h // 2 + 2 < 16:
                p2 = h // 2 + 2
                qr_n[p2] = dma("sync", f"yc{p2 % 8}", qrfm3[:, p2, :],
                               QRS[:, p2, :])
            if h < 16:
                cast(WOUTC[h], WOUT[h])
                cast(B2C[h], B2[h])

        def emit_qlat(h, banks):
            nonlocal qlat_free
            hs = h % 2
            h4 = h % 4
            nq, nk = load_done[h]
            q3 = qlat3P[hs]
            kb.wait("tensor", f"yq{hs}", nq)
            kb.wait("tensor", f"gk{h4}", nk)
            if qlat_free:
                kb.wait("tensor", qlat_free[0], qlat_free[1])
            lastev = {}
            vm = sm = None
            for rc in range(4):
                for qh in range(2):
                    bk = banks[(rc * 2 + qh) % 4]
                    pv = lastev.get(bk)
                    if pv:
                        kb.wait("tensor", pv[0], pv[1])
                    else:
                        kb.guard(bk)
                    tmq = kb.t(lambda e, o=bank(bk), hs=hs, h4=h4,
                               l=wukb4[h4][:, rc * 128:(rc + 1) * 128],
                               r=qnopeb[hs][:, qh * 512:(qh + 1) * 512]:
                               e.matmul(o, l, r, start=True, stop=True))
                    eng = "vector" if qh == 0 else "scalar"
                    enm = "V" if qh == 0 else "S"
                    kb.wait(eng, "T", tmq)
                    mv = kb.op(eng, lambda e, d=q3[:, rc, qh * 512:(qh + 1) * 512],
                               s=bank(bk): SMUL(e, d, s, SCALE),
                               inc=(enm, 1))
                    lastev[bk] = (enm, mv)
                    kb.release(bk, enm, mv)
                    if enm == "V":
                        vm = mv
                    else:
                        sm = mv
            qlat_free = lastev[banks[3]]
            qn_free[hs] = kb.cnt["T"]
            return (vm, sm)

        def emit_scores(h, qt, g, ng, par):
            d = qt + 1
            hb = 64 * (h % 2)
            hp = h // 2
            if hp not in qr_seen:
                qr_seen.add(hp)
                kb.wait("tensor", f"yc{hp % 8}", qr_n[hp])
            q3 = qlat3P[h % 2]
            kbs = [2 * g] + ([2 * g + 1] if 2 * g + 1 < d else [])
            pp = pair(par)
            rel = sc_free[par]
            if rel:
                kb.wait("tensor", "V", rel[0])
                kb.wait("tensor", "S", rel[1])
            kb.guard(2 * par, 2 * par + 1)
            fns = []
            for rc in range(4):
                for j, kbi in enumerate(kbs):
                    fns.append(lambda e, o=pp[:, j * 512:(j + 1) * 512],
                               l=q3[:, rc, qt * 128:(qt + 1) * 128],
                               r=ckvfm3[:, rc, kbi * 512:(kbi + 1) * 512],
                               st=(rc == 0): e.matmul(o, l, r, start=st, stop=False))
            for j, kbi in enumerate(kbs):
                diag = (kbi == d - 1)
                fns.append(lambda e, o=pp[:, j * 512:(j + 1) * 512],
                           l=qrfm3[hb:hb + 64, h // 2, qt * 128:(qt + 1) * 128],
                           r=krdup[hb:hb + 64, kbi * 512:(kbi + 1) * 512],
                           sp=(not diag): e.matmul(o, l, r, start=False, stop=sp))
                if diag:
                    fns.append(lambda e, o=pp[:, j * 512:(j + 1) * 512]:
                               e.matmul(o, mU[:], mV[:], start=False, stop=True))
            return t_run(fns), len(kbs)

        def emit_tail_a_vs(qt, g, ng, par, t_sc, nkb, i4x):
            pp = pair(par)
            wid = 512 * nkb
            kb.wait("vector", "T", t_sc)
            if g == 0:
                kb.drain("vector")
                v_mx = kb.v(lambda e, b=nm4[i4x]: e.reduce_max(
                    b, pp[:, 0:wid], AX, negate=True))
                cur = nm4[i4x]
            else:
                kb.drain("vector")
                kb.raw("vector", lambda e: e.reduce_max(
                    ST["negmg"], pp[:, 0:wid], AX, negate=True))
                kb.drain("vector")
                kb.raw("vector", lambda e, a=nm4[(i4x + 2) % 4], b=nm4[i4x]:
                       e.tensor_tensor(b, a, ST["negmg"], ALU.min))
                v_mx = kb.v(lambda e, a=nm4[(i4x + 2) % 4], f=fd4[i4x]:
                            e.scalar_tensor_tensor(
                                f, ST["negmg"], a, ST["zero"],
                                ALU.subtract, ALU.min))
                cur = nm4[i4x]
            kb.wait("scalar", "V", v_mx)
            if g > 0:
                kb.raw("scalar", lambda e, f=fd4[i4x], o=fP4[i4x]:
                       e.activation(o, f, ACT.Exp))
            s_exp = kb.s(lambda e, o=probsb[i4x][:, 0:wid], i=pp[:, 0:wid],
                         b=cur, l=lgP4[i4x]:
                         e.activation(o, i, ACT.Exp, bias=b, accum_out=l))
            sc_free[par] = (v_mx, s_exp)
            return s_exp

        def emit_tail_a_t(qt, g, ng, par, s_exp, nkb, pt4):
            nonlocal tr_free
            wid = 512 * nkb
            kb.wait("tensor", "S", s_exp)
            if tr_free:
                kb.wait("tensor", tr_free[0], tr_free[1])
            fns = []
            for tcn in range(4 * nkb):
                fns.append(lambda e, o=bankb(5)[:, tcn * 128:(tcn + 1) * 128],
                           i=probsb[pt4][:, tcn * 128:(tcn + 1) * 128]:
                           e.transpose(o, i, idb[:]))
            t_tr = t_run(fns)
            eng = "vector" if par == 0 else "scalar"
            enm = "V" if par == 0 else "S"
            kb.wait(eng, "T", t_tr)
            mv = kb.op(eng, lambda e, d=probsT[pt4][:, 0:wid], s=bankb(5)[:, 0:wid]:
                       CP(e, d, s), inc=(enm, 1))
            tr_free = (enm, mv)
            return mv, enm

        def emit_tail_b(qt, g, ng, par, nkb, ev, qp, pt4):
            nonlocal pv_free
            mv, enm = ev
            kb.wait("tensor", enm, mv)
            if c_pre_pend:
                kb.wait("tensor", "ya", c_pre_pend.pop())
            if pv_free:
                kb.wait("tensor", "V", pv_free)
            fns = []
            for tcn in range(4 * nkb):
                tci = 8 * g + tcn
                fns.append(lambda e, o=bank(4),
                           l=probsT[pt4][:, tcn * 128:(tcn + 1) * 128],
                           r=ckvtm3[:, tci, :], st=(tcn == 0), sp=(tcn == 4 * nkb - 1):
                           e.matmul(o, l, r, start=st, stop=sp))
            t_pv = t_run(fns)
            kb.wait("vector", "T", t_pv)
            if g == 0:
                rel = olat_free.get(qp)
                if rel:
                    kb.wait("vector", rel[0], rel[1])
                kb.raw("vector", lambda e, qp=qp: e.tensor_copy(accb4[qp][:], bank(4)))
                v_mg = kb.v(lambda e, l=lgP4[pt4], qp=qp: e.tensor_copy(laccP[qp], l))
            else:
                kb.raw("vector", lambda e, f=fP4[pt4], qp=qp: e.scalar_tensor_tensor(
                    accb4[qp][:], accb4[qp][:], f, bank(4), ALU.mult, ALU.add))
                v_mg = kb.v(lambda e, f=fP4[pt4], l=lgP4[pt4], qp=qp: e.scalar_tensor_tensor(
                    laccP[qp], laccP[qp], f, l, ALU.mult, ALU.add))
            pv_free = v_mg
            return v_mg

        def emit_final_a_v(qt, ng, qp):
            kb.drain("vector")
            kb.raw("vector", lambda e, qp=qp: e.reciprocal(linvP[qp], laccP[qp]))
            kb.drain("vector")
            rel = olat_free.get(qp)
            if rel:
                kb.wait("vector", rel[0], rel[1])
            v_ol = kb.v(lambda e, qp=qp: e.tensor_scalar(
                olatb4[qp][:], accb4[qp][:], linvP[qp], None, ALU.mult))
            return v_ol

        def emit_final_a_t(qt, ng, qp, v_ol):
            nonlocal tr7_free
            kb.wait("tensor", "V", v_ol)
            if tr7_free:
                kb.wait("tensor", tr7_free[0], tr7_free[1])
            fns = []
            for rc in range(4):
                fns.append(lambda e, o=bankb(7)[:, rc * 128:(rc + 1) * 128],
                           i=olatb4[qp][:, rc * 128:(rc + 1) * 128]:
                           e.transpose(o, i, idb[:]))
            t_ot = t_run(fns)
            olat_free[qp] = ("T", t_ot)
            kb.wait("scalar", "T", t_ot)
            rel = olT_free.get(qp)
            if rel:
                kb.wait("scalar", rel[0], rel[1])
            s_ote = kb.s(lambda e, qp=qp: CP(e, olatT4[qp][:], bankb(7)[:, 0:512]))
            tr7_free = ("S", s_ote)
            return s_ote

        def emit_final_b(h, qt, ng, qp, s_ote):
            nonlocal oh_free
            hs = h % 2
            h4 = h % 4
            kb.wait("tensor", "S", s_ote)
            if oh_free:
                kb.wait("tensor", "S", oh_free)
            kb.guard(6)
            fns = []
            for rc in range(4):
                fns.append(lambda e, o=bank(6)[:, 0:128], h4=h4,
                           l=view3(wuvb4[h4][:], 128)[:, rc, :],
                           r=olatT4[qp][:, rc * 128:(rc + 1) * 128],
                           st=(rc == 0), sp=(rc == 3):
                           e.matmul(o, l, r, start=st, stop=sp))
            t_oh = t_run(fns)
            olT_free[qp] = ("T", t_oh)
            kb.wait("scalar", "T", t_oh)
            kb.wait("scalar", f"yat{hs}", 16 * uses.get(f"yat{hs}", 0))
            s_at = kb.s(lambda e, hs=hs, d=attnstg[hs][:, qt * 128:(qt + 1) * 128]:
                        CP(e, d, bank(6)[:, 0:128]))
            kb.release(6, "S", s_at)
            oh_free = s_at
            return s_at

        issue_loads(0)
        issue_loads(1)
        qlat_ready = {0: emit_qlat(0, [6, 7, 6, 7]), 1: emit_qlat(1, [6, 7, 6, 7])}

        def emit_qlat_part(h, rc, par):
            hs = h % 2
            h4 = h % 4
            nq, nk = load_done[h]
            q3 = qlat3P[hs]
            kb.wait("tensor", f"yq{hs}", nq)
            kb.wait("tensor", f"gk{h4}", nk)
            rel = sc_free[par]
            if rel:
                kb.wait("tensor", "V", rel[0])
                kb.wait("tensor", "S", rel[1])
            kb.guard(2 * par, 2 * par + 1)
            fns = []
            for qh in range(2):
                fns.append(lambda e, o=pair(par)[:, qh * 512:(qh + 1) * 512],
                           hs=hs, h4=h4, rc=rc,
                           l=wukb4[h4][:, rc * 128:(rc + 1) * 128],
                           r=qnopeb[hs][:, qh * 512:(qh + 1) * 512]:
                           e.matmul(o, l, r, start=True, stop=True))
            tmq = t_run(fns)
            eng = "vector" if par == 0 else "scalar"
            enm = "V" if par == 0 else "S"
            kb.wait(eng, "T", tmq)
            mv = kb.op(eng, lambda e, d=q3[:, rc, :], s=pair(par):
                       SMUL(e, d, s, SCALE), inc=(enm, 1))
            if enm == "V":
                sc_free[par] = (mv, kb.cnt["S"])
            else:
                sc_free[par] = (kb.cnt["V"], mv)
            if rc == 3:
                qn_free[hs] = kb.cnt["T"]
            return mv

        stream = []
        load_hooks = {}
        qlat_mile = {}
        for hp in range(H_ // 2):
            h0 = 2 * hp
            for k in range(NU1):
                stream.append(("U", h0) + units_base[k])
                stream.append(("U", h0 + 1) + units_base[k])
            if h0 + 2 < H_:
                load_hooks[len(stream) - 12] = h0 + 2
                load_hooks[len(stream) - 8] = h0 + 3
                for r in range(4):
                    stream.append(("Q", h0 + 2, r))
                for r in range(4):
                    stream.append(("Q", h0 + 3, r))
        NS = len(stream)

        st_sc = {}
        st_exp = {}
        st_ev = {}
        st_fv = {}
        st_fin = {}
        fin_count = {h: sum(1 for u in units_base if u[1] == u[2] - 1) for h in range(H_)}

        for hh in (0, 1):
            kb.wait("tensor", "V", qlat_ready[hh][0])
            kb.wait("tensor", "S", qlat_ready[hh][1])

        def is_final_i(i):
            e = stream[i]
            return e[0] == "U" and e[3] == e[4] - 1

        def qpx(i):
            e = stream[i]
            return (e[1] % 2) * 2 + e[2] % 2

        for tick in range(NS + 5):
            if tick in load_hooks:
                issue_loads(load_hooks[tick])
            i = tick - 1
            if 0 <= i < NS and i in st_sc:
                hh, qt, g, ng, par, t_sc, nkb = st_sc[i]
                st_exp[i] = emit_tail_a_vs(qt, g, ng, par, t_sc, nkb, i % 4)
            i = tick - 4
            if 0 <= i < NS and is_final_i(i):
                _, hh, qt, g, ng = stream[i]
                st_fv[i] = emit_final_a_v(qt, ng, qpx(i))
            if tick < NS:
                e = stream[tick]
                par = tick % 2
                if e[0] == "U":
                    _, hh, qt, g, ng = e
                    t_sc, nkb = emit_scores(hh, qt, g, ng, par)
                    st_sc[tick] = (hh, qt, g, ng, par, t_sc, nkb)
                else:
                    _, hh, rc = e
                    emit_qlat_part(hh, rc, par)
            i = tick - 3
            if 0 <= i < NS and i in st_sc:
                hh, qt, g, ng, par, t_sc, nkb = st_sc.pop(i)
                emit_tail_b(qt, g, ng, par, nkb, st_ev.pop(i), (hh % 2) * 2 + qt % 2, i % 4)
            i = tick - 2
            if 0 <= i < NS and i in st_sc:
                hh, qt, g, ng, par, t_sc, nkb = st_sc[i]
                st_ev[i] = emit_tail_a_t(qt, g, ng, par, st_exp.pop(i), nkb, i % 4)
            i = tick - 4
            if 0 <= i < NS and is_final_i(i):
                _, hh, qt, g, ng = stream[i]
                st_fin[i] = emit_final_a_t(qt, ng, qpx(i), st_fv.pop(i))
            i = tick - 5
            if 0 <= i < NS and is_final_i(i):
                _, hh, qt, g, ng = stream[i]
                s_at = emit_final_b(hh, qt, ng, qpx(i), st_fin.pop(i))
                fin_count[hh] -= 1
                if fin_count[hh] == 0:
                    kb.wait("sync", "S", s_at)
                    dma("sync", f"yat{hh % 2}", ATS[:, hh, :], attnstg[hh % 2][:])
                    wk_free4[hh % 4] = kb.cnt["T"]

        c_yat0 = 16 * uses.get("yat0", 0)
        c_yat1 = 16 * uses.get("yat1", 0)
        stage_barrier()

        attn3 = view3(attn_cat[:], 1024)
        hg3 = view3(hgD[:], 1024)
        kb.wait("sync", "yat0", c_yat0)
        kb.wait("sync", "yat1", c_yat1)
        kb.wait("sync", "ysb", b_ysb)
        yc_n = {}
        for ai in range(8):
            yc_n[ai] = dma("sync", f"yc{ai}", attn3[:, 4 * ai:4 * (ai + 1), :],
                           ATS[:, 4 * ai:4 * (ai + 1), :])
        d_hg_pend = True

        gw_free = {0: 0, 1: 0}
        xd_free = {0: 0, 1: 0, 2: 0, 3: 0}
        dsb_free = {0: None, 1: None, 2: None, 3: None}
        ou_free = {0: 0, 1: 0}
        mm_free = {0: None, 1: None}
        dpend = []
        ob_ctr = [0]

        def flush_dpend(keep):
            while len(dpend) > keep:
                sl4, th, v_d, nxd, dt0, oc0 = dpend.pop(0)
                ob = ob_ctr[0] % 2
                ob_ctr[0] += 1
                kb.wait("tensor", "V", v_d)
                kb.guard(4 + th)
                fns = []
                for i4 in range(4):
                    fns.append(lambda e, o=bank(4 + th)[:, i4 * 128:(i4 + 1) * 128],
                               i=dsb[sl4][:, i4 * 128:(i4 + 1) * 128]:
                               e.transpose(o, i, idf[:]))
                t_tr = t_run(fns)
                dsb_free[sl4] = ("T", t_tr)
                kb.wait("vector", "T", t_tr)
                kb.wait("vector", f"yd{sl4}", nxd)
                kb.wait("vector", f"yo{ob}", ou_free[ob])
                v_o = kb.v(lambda e, o=outbf[ob][:], x=xdb[sl4][:], s=bank(4 + th):
                           e.tensor_tensor(o, s, x, ALU.add))
                xd_free[sl4] = v_o
                kb.release(4 + th, "V", v_o)
                kb.wait("sync", "V", v_o)
                odst = OUT[th * 512:(th + 1) * 512,
                           dt0 * 256 + oc0 * 128: dt0 * 256 + (oc0 + 1) * 128].rearrange(
                    "(i p) j -> p i j", p=128)
                ou_free[ob] = dma("sync", f"yo{ob}", odst, view3(outbf[ob][:], 128))

        un = 0
        for dt in range(16):
            ws = dt % 2
            gw = f"gw{ws}"
            if dt == 0:
                dma("sync", "ya", woutb[0][:, 0:4096], WOUTC[0][:, 0:4096])
                dma("sync", "ya", woutb[0][:, 4096:8192], WOUTC[0][:, 4096:8192])
                ngw = dma("sync", "ya", b2b[0][:], B2C[0])
                dma("sync", "ysb", hg3[:, 0:4, :], HGS[:, 0:4, :])
                d_pre = dma("sync", "ysb", hg3[:, 4:8, :], HGS[:, 4:8, :])
                kb.wait("tensor", "ya", ngw)
            else:
                kb.wait("gpsimd", "T", gw_free[ws])
                dma("gpsimd", gw, woutb[ws][:], WOUTC[dt])
                ngw = dma("gpsimd", gw, b2b[ws][:], B2C[dt])
                kb.wait("tensor", gw, ngw)
            for oc in range(2):
                pp = pair(oc)
                if mm_free[oc]:
                    kb.wait("tensor", mm_free[oc][0], mm_free[oc][1])
                kb.guard(2 * oc, 2 * oc + 1)
                fns = []
                for fc in range(32):
                    if dt == 0 and oc == 0 and fc % 4 == 0:
                        for f2 in fns:
                            kb.raw("tensor", f2)
                        fns = []
                        kb.wait("tensor", f"yc{fc // 4}", yc_n[fc // 4])
                    for th in range(2):
                        fns.append(lambda e, o=pp[:, th * 512:(th + 1) * 512],
                                   l=view3(woutb[ws][:], 256)[:, fc, oc * 128:(oc + 1) * 128],
                                   r=attn3[:, fc, th * 512:(th + 1) * 512],
                                   st=(fc == 0): e.matmul(o, l, r, start=st, stop=False))
                if dt == 0 and oc == 0:
                    kb.wait("tensor", "ysb", d_pre)
                for rc in range(8):
                    for th in range(2):
                        fns.append(lambda e, o=pp[:, th * 512:(th + 1) * 512],
                                   l=view3(b2b[ws][:], 256)[:, rc, oc * 128:(oc + 1) * 128],
                                   r=hg3[:, rc, th * 512:(th + 1) * 512],
                                   sp=(rc == 7): e.matmul(o, l, r, start=False, stop=sp))
                tmm = t_run(fns)
                last_v = None
                for th in range(2):
                    sl4 = un % 4
                    un += 1
                    kb.wait("sync", "V", xd_free[sl4])
                    nxd = dma("sync", f"yd{sl4}", view3(xdb[sl4][:], 128),
                              XOWN[th * 512:(th + 1) * 512,
                                   dt * 256 + oc * 128: dt * 256 + (oc + 1) * 128].rearrange(
                                  "(i p) j -> p i j", p=128))
                    kb.wait("vector", "T", tmm)
                    if dsb_free[sl4]:
                        kb.wait("vector", dsb_free[sl4][0], dsb_free[sl4][1])
                    v_d = kb.v(lambda e, d=dsb[sl4][:], s=pp[:, th * 512:(th + 1) * 512]:
                               e.tensor_copy(d, s))
                    last_v = v_d
                    dpend.append((sl4, th, v_d, nxd, dt, oc))
                mm_free[oc] = ("V", last_v)
                flush_dpend(2)
            gw_free[ws] = kb.cnt["T"]

        flush_dpend(0)
        kb.wait("sync", "yo0", 16 * uses.get("yo0", 0))
        kb.wait("sync", "yo1", 16 * uses.get("yo1", 0))

        with nc.Block() as blk:
            @blk.tensor
            def _(e):
                for f in kb.ops["tensor"]:
                    f(e)

            @blk.vector
            def _(e):
                for f in kb.ops["vector"]:
                    f(e)

            @blk.scalar
            def _(e):
                for f in kb.ops["scalar"]:
                    f(e)

            @blk.gpsimd
            def _(e):
                for f in kb.ops["gpsimd"]:
                    f(e)

            @blk.sync
            def _(e):
                for f in kb.ops["sync"]:
                    f(e)

    return nc


def _host_tables():
    DRh = DR_ // 2
    freqs = (1.0 / (10000.0 ** (np.arange(0, DR_, 2, dtype=np.float32) / np.float32(DR_)))).astype(np.float32)
    wavelengths = (2.0 * np.pi / freqs).astype(np.float32)
    ramp = np.clip((wavelengths / 4096.0 - 1.0) / (32.0 - 1.0), 0.0, 1.0).astype(np.float32)
    scale = (1.0 - ramp + ramp * 40.0).astype(np.float32)
    inv = (freqs / scale).astype(np.float32)
    pos = np.arange(S_, dtype=np.float32)
    f = pos[:, None] * inv[None, :]
    f = np.concatenate([f, f], axis=1).astype(np.float32)
    cos = np.cos(f).astype(np.float32)
    sin = np.sin(f).astype(np.float32)
    sin_signed = sin.copy()
    sin_signed[:, :DRh] = -sin[:, :DRh]
    return cos, sin_signed


def _core_rows(c):
    g = c % 4
    return np.concatenate([np.arange((4 * d + g) * 128, (4 * d + g) * 128 + 128)
                           for d in range(8)])


def kernel(L0_attn_hidden_states, L0_attn_w_q_down, L0_attn_w_q_up_nope,
           L0_attn_w_q_up_rope, L0_attn_w_kv_down, L0_attn_w_k_rope,
           L0_attn_w_uk, L0_attn_w_uv, L0_attn_w_out,
           L0_ffn_A, L0_ffn_B, L0_ffn_Wr):
    x = np.ascontiguousarray(L0_attn_hidden_states, dtype=np.float32)
    wqd0 = np.asarray(L0_attn_w_q_down, dtype=np.float32)
    wqn0 = np.asarray(L0_attn_w_q_up_nope, dtype=np.float32)
    wqr0 = np.asarray(L0_attn_w_q_up_rope, dtype=np.float32)
    wkv0 = np.asarray(L0_attn_w_kv_down, dtype=np.float32)
    wkr0 = np.asarray(L0_attn_w_k_rope, dtype=np.float32)
    wuk = np.ascontiguousarray(np.asarray(L0_attn_w_uk, dtype=np.float32))
    wout0 = np.asarray(L0_attn_w_out, dtype=np.float32)
    A20 = np.asarray(L0_ffn_A, dtype=np.float32).transpose(2, 1, 0).reshape(D_, RK_ * NE_)
    B20 = np.asarray(L0_ffn_B, dtype=np.float32).transpose(2, 0, 1).reshape(RK_ * NE_, D_)
    wrt0 = np.asarray(L0_ffn_Wr, dtype=np.float32).T
    wqd = np.ascontiguousarray(
        wqd0.reshape(32, 128, 12, 128).transpose(2, 1, 0, 3).reshape(12, 128, 4096))
    wqn = np.ascontiguousarray(
        wqn0.reshape(12, 128, 32, 128).transpose(2, 1, 0, 3).reshape(32, 128, 1536))
    wqr = np.ascontiguousarray(
        wqr0.reshape(2, 6, 128, 2048).transpose(0, 2, 1, 3).reshape(2, 128, 12288))
    wkvc = np.ascontiguousarray(
        np.concatenate([wkv0, wkr0], axis=1).reshape(32, 128, 576).transpose(1, 0, 2))
    wuvT = np.ascontiguousarray(
        np.asarray(L0_attn_w_uv, dtype=np.float32).reshape(H_, DV_, KVL_)
        .transpose(0, 2, 1).reshape(H_, 4, 128, 128).transpose(0, 2, 1, 3)
        .reshape(H_, 128, 512))
    wout = np.ascontiguousarray(
        wout0.reshape(32, 128, 16, 256).transpose(2, 1, 0, 3).reshape(16, 128, 8192))
    A2 = np.ascontiguousarray(
        A20.reshape(32, 128, 8, 128).transpose(2, 1, 0, 3).reshape(8, 128, 4096))
    B2 = np.ascontiguousarray(
        B20.reshape(8, 128, 16, 256).transpose(2, 1, 0, 3).reshape(16, 128, 2048))
    wrt = np.ascontiguousarray(
        wrt0.reshape(32, 128, 64).transpose(1, 0, 2).reshape(128, 2048))

    cos, sin_s = _host_tables()
    idb = np.eye(128, dtype=bf16)
    idf = np.eye(128, dtype=np.float32)
    mu = np.ascontiguousarray(np.tril(np.ones((128, 128), dtype=np.float32), -1).astype(bf16))

    nc = build_nc()

    in_maps = []
    for c in range(8):
        b = c // 4
        g = c % 4
        rows = _core_rows(c)
        mv = np.zeros((128, 512), dtype=np.float32)
        jj = np.arange(128)[:, None]
        cc = np.arange(512)[None, :]
        mv[(cc >= jj + g * 128)] = NEG_BIG
        in_maps.append({
            "xkv": np.ascontiguousarray(x[b]),
            "xown": np.ascontiguousarray(x[b][rows]),
            "wqd": wqd, "wqn": wqn, "wqr": wqr, "wkvc": wkvc,
            "wuk": wuk, "wuvT": wuvT, "wout": wout,
            "a2": A2, "b2": B2, "wrt": wrt,
            "c_idb": idb, "c_idf": idf, "c_mu": mu,
            "c_mv": np.ascontiguousarray(mv.astype(bf16)),
            "cosq": np.ascontiguousarray((cos[rows] * SCALE).astype(np.float32)),
            "sinq": np.ascontiguousarray((sin_s[rows] * SCALE).astype(np.float32)),
            "cosk": np.ascontiguousarray(cos),
            "sink": np.ascontiguousarray(sin_s),
        })

    res = run_bass_kernel_spmd(nc, in_maps, core_ids=list(range(8)))

    out = np.empty((B_, S_, D_), dtype=np.float32)
    for c in range(8):
        out[c // 4, _core_rows(c), :] = res.results[c]["out_own"]
    return out


# version 34 (final)
# speedup vs baseline: 1.0072x; 1.0072x over previous
import math
from contextlib import ExitStack

import numpy as np
import ml_dtypes

import concourse.bass as bass
import concourse.mybir as mybir
from concourse.bass_utils import run_bass_kernel_spmd

BF16 = mybir.dt.bfloat16
F32 = mybir.dt.float32
ALU = mybir.AluOpType
ACT = mybir.ActivationFunctionType
AX = mybir.AxisListType.X

B_, S_, D_ = 2, 4096, 4096
H_, QL_, KVL_, DN_, DR_, DV_ = 32, 1536, 512, 128, 64, 128
NE_, RK_ = 64, 16
NTOK = 1024
SCALE = 1.0 / math.sqrt(float(DN_ + DR_))
NEG_BIG = -1.0e9
EPS = 1e-5

bf16 = ml_dtypes.bfloat16


class KB:
    ENGS = ("tensor", "vector", "scalar", "gpsimd", "sync")

    def __init__(self, sems):
        self.ops = {e: [] for e in self.ENGS}
        self.sems = sems
        self.cnt = {k: 0 for k in sems}
        self.bankrel = {}

    def raw(self, eng, fn):
        self.ops[eng].append(fn)
        return None

    def op(self, eng, fn, inc=None):
        if inc is None:
            self.ops[eng].append(fn)
            return None
        nm, amt = inc
        self.cnt[nm] += amt
        h = self.sems[nm]

        def wrapped(e, fn=fn, h=h, amt=amt):
            fn(e).then_inc(h, amt)
        self.ops[eng].append(wrapped)
        return self.cnt[nm]

    def wait(self, eng, nm, val):
        if val is None or val <= 0:
            return
        h = self.sems[nm]
        self.ops[eng].append(lambda e, h=h, v=val: e.wait_ge(h, v))

    def drain(self, eng):
        self.ops[eng].append(lambda e: e.drain())

    def t(self, fn):
        return self.op("tensor", fn, inc=("T", 1))

    def v(self, fn):
        return self.op("vector", fn, inc=("V", 1))

    def s(self, fn):
        return self.op("scalar", fn, inc=("S", 1))

    def guard(self, *banks):
        for bk in banks:
            rel = self.bankrel.get(bk)
            if rel:
                self.wait("tensor", rel[0], rel[1])

    def release(self, bk, sem_name, milestone):
        self.bankrel[bk] = (sem_name, milestone)



def CP(e, d, s):
    if hasattr(e, "tensor_copy"):
        return e.tensor_copy(d, s)
    return e.copy(d, s)


def SMUL(e, d, s, c):
    if hasattr(e, "tensor_scalar"):
        return e.tensor_scalar(d, s, c, None, ALU.mult)
    return e.mul(d, s, c)


def view3(t_ap, inner):
    n = t_ap.shape[1] // inner
    return t_ap.rearrange("p (c j) -> p c j", c=n)


def build_nc():
    nc = bass.Bass("TRN2", target_bir_lowering=False)

    def din(name, shape, dtype=F32):
        return nc.dram_tensor(name, list(shape), dtype, kind="ExternalInput").ap()

    XKV = din("xkv", (S_, D_))
    XOWN = din("xown", (NTOK, D_))
    WQD = din("wqd", (12, 128, 4096))
    WQN = din("wqn", (32, 128, 1536))
    WQR = din("wqr", (2, 128, 12288))
    WKVC = din("wkvc", (128, 32, 576))
    WUK = din("wuk", (D_, KVL_))
    WUVT = din("wuvT", (H_, 128, 512))
    WOUT = din("wout", (16, 128, 8192))
    A2 = din("a2", (8, 128, 4096))
    B2 = din("b2", (16, 128, 2048))
    WRT = din("wrt", (128, 2048))
    C_IDB = din("c_idb", (128, 128), BF16)
    C_IDF = din("c_idf", (128, 128), F32)
    C_MU = din("c_mu", (128, 128), BF16)
    C_MV = din("c_mv", (128, 512), BF16)
    COSQ = din("cosq", (NTOK, DR_))
    SINQ = din("sinq", (NTOK, DR_))
    COSK = din("cosk", (S_, DR_))
    SINK = din("sink", (S_, DR_))

    QNS = nc.dram_tensor("qnope_scr", [128, H_, NTOK], BF16, kind="Internal").ap()
    QRS = nc.dram_tensor("qrope_scr", [128, 16, NTOK], BF16, kind="Internal").ap()
    HGS = nc.dram_tensor("hg_scr", [128, 8, NTOK], BF16, kind="Internal").ap()
    KTS = nc.dram_tensor("ckvtm_scr", [128, S_ // 128, KVL_], BF16, kind="Internal").ap()
    ATS = nc.dram_tensor("attn_scr", [128, H_, NTOK], BF16, kind="Internal").ap()
    WOUTC = nc.dram_tensor("wout_c", [16, 128, 8192], BF16, kind="Internal").ap()
    WKVCC = nc.dram_tensor("wkvc_c", [128, 18432], BF16, kind="Internal").ap()
    B2C = nc.dram_tensor("b2_c", [16, 128, 2048], BF16, kind="Internal").ap()
    OUT = nc.dram_tensor("out_own", [NTOK, D_], F32, kind="ExternalOutput").ap()

    es = ExitStack()
    with es:
        sem_names = ["T", "V", "S", "P", "yx0", "yx1", "ya", "ysb", "ysa",
                     "yq0", "yq1", "yat0", "yat1", "yt0", "yt1",
                     "g0", "g1", "gk0", "gk1", "gk2", "gk3", "gw0", "gw1",
                     "yd0", "yd1", "yd2", "yd3", "yo0", "yo1", "gc", "gq", "yx2",
                     "yc0", "yc1", "yc2", "yc3", "yc4", "yc5", "yc6", "yc7"]
        sems = {nm: es.enter_context(nc.semaphore(nm)) for nm in sem_names}
        kb = KB(sems)

        def sbuf(name, shape, dtype):
            return es.enter_context(nc.sbuf_tensor(name, list(shape), dtype))

        idb = sbuf("idb", (128, 128), BF16)
        idf = sbuf("idf", (128, 128), F32)
        mU = sbuf("mU", (128, 128), BF16)
        mV = sbuf("mV", (128, 512), BF16)
        xn_tm = [sbuf("xn_tm0", (128, 4096), BF16), sbuf("xn_tm1", (128, 4096), BF16)]
        sqd = sbuf("sqd", (128, 2048), BF16)
        stats = sbuf("stats", (128, 64), F32)
        top16 = sbuf("top16", (128, 16), F32)
        top16e = sbuf("top16e", (128, 16), F32)

        ST = {}
        for i, nm in enumerate(["xsum", "xsqa", "xsqb", "negm", "m2", "ex2", "varv",
                                "lnv", "rstd", "nmr", "negm16", "l16", "linv16",
                                "scl2", "negmg", "negmA", "negmB", "fdA", "fdB",
                                "fA", "fB", "lgA", "lgB", "lacc", "linv",
                                "lacc1", "linv1", "fC", "fD", "lgC", "lgD", "zero",
                                "negmC", "negmD", "fdC", "fdD",
                                "lacc2", "lacc3", "linv2", "linv3", "xsum2",
                                "xsumZ", "xsum2Z", "xsqaZ", "xsqbZ", "negmZ",
                                "m2Z", "ex2Z", "varvZ", "lnvZ", "rstdZ", "nmrZ"]):
            ST[nm] = stats[:, i:i + 1]

        P1B, P2B, PFB = 96256, 40960, 39680
        pool1 = sbuf("pool1", (128, P1B), mybir.dt.uint8)
        pool2 = sbuf("pool2", (128, P2B), mybir.dt.uint8)
        poolf = sbuf("poolf", (128, PFB), mybir.dt.uint8)

        def pview(pool, name, shape, dtype, off):
            nbytes = int(np.prod(shape[1:])) * mybir.dt.size(dtype)
            assert off % 32 == 0, name
            assert off + nbytes <= pool.shape[1], (name, off, nbytes)
            base = nc.lookup_mloc(pool).addr
            return nc.alloc_sbuf_tensor_at(name, list(shape), dtype,
                                           offset=base + off)

        xnfmB = pview(pool1, "xnfmB", (128, 32768), BF16, 0)
        cqfm = pview(pool1, "cqfm", (128, 12288), BF16, 65536)
        qn_stg = [pview(pool1, "qn_stg0", (128, 1024), BF16, 90112),
                  pview(pool1, "qn_stg1", (128, 1024), BF16, 92160)]
        tr_stg = [pview(pool1, "tr_stg0", (128, 512), BF16, 94208),
                  pview(pool1, "tr_stg1", (128, 512), BF16, 95232)]
        wqrb = pview(pool1, "wqrb", (128, 24576), BF16, 0)
        qro_tm = [pview(pool1, "qro0", (128, 2048), BF16, 49152),
                  pview(pool1, "qro1", (128, 2048), BF16, 53248)]
        slabA2 = [pview(pool1, "slabA0", (128, 8192), BF16, 0),
                  pview(pool1, "slabA1", (128, 8192), BF16, 16384)]
        wkvcb = pview(pool1, "wkvcb", (128, 18432), BF16, 32768)
        ktm_stg = [pview(pool1, "ktm0", (128, 2048), BF16, 69632),
                   pview(pool1, "ktm1", (128, 2048), BF16, 73728)]
        krtm2 = [pview(pool1, "krtm", (128, 256), BF16, 77824),
                 pview(pool1, "krtmB", (128, 256), BF16, 78336)]
        ckvtm = pview(pool1, "ckvtm", (128, 16384), BF16, 0)
        qropefm = pview(pool1, "qropefm", (128, 16384), BF16, 32768)
        qlat = pview(pool1, "qlat", (128, 4096), BF16, 65536)
        probsb = [pview(pool1, "probsb0", (128, 1024), BF16, 73728),
                  pview(pool1, "probsb1", (128, 1024), BF16, 75776),
                  None, None]
        probsT = [pview(pool1, "probsT0", (128, 1024), BF16, 77824),
                  pview(pool1, "probsT1", (128, 1024), BF16, 79872),
                  None, None]
        olatb = pview(pool1, "olatb", (128, 512), BF16, 81920)
        olatT = pview(pool1, "olatT", (128, 512), BF16, 82944)
        attnstg = [pview(pool1, "attnstg0", (128, 1024), BF16, 83968),
                   pview(pool1, "attnstg1", (128, 1024), BF16, 86016)]
        qnopeb = [pview(pool1, "qnopeb0", (128, 1024), BF16, 88064),
                  pview(pool1, "qnopeb1", (128, 1024), BF16, 90112)]
        wukb = [pview(pool1, "wukb0", (128, 512), BF16, 92160),
                pview(pool1, "wukb1", (128, 512), BF16, 93184)]
        wuvb = [pview(pool1, "wuvb0", (128, 512), BF16, 94208),
                pview(pool1, "wuvb1", (128, 512), BF16, 95232)]
        attn_cat = pview(pool1, "attn_cat", (128, 32768), BF16, 0)
        b2b = [pview(pool1, "b2b0", (128, 2048), BF16, 65536),
               pview(pool1, "b2b1", (128, 2048), BF16, 69632)]
        hgD = pview(pool1, "hgD", (128, 8192), BF16, 73728)

        wbuf = [pview(pool2, "wbuf0", (128, 4096), BF16, 0),
                pview(pool2, "wbuf1", (128, 4096), BF16, 8192)]
        ckvfm = pview(pool2, "ckvfm", (128, 16384), BF16, 0)
        krdup = pview(pool2, "krdup", (128, 4096), BF16, 32768)
        woutb = [pview(pool2, "woutb0", (128, 8192), BF16, 0),
                 pview(pool2, "woutb1", (128, 8192), BF16, 16384)]

        xbuf = [pview(poolf, "xbuf0", (128, 4096), F32, 0),
                pview(poolf, "xbuf1", (128, 4096), F32, 16384),
                pview(pool2, "xbuf2", (128, 4096), F32, 16384)]
        ctab = [pview(poolf, "ctab0", (128, 64), F32, 16384),
                pview(poolf, "ctab1", (128, 64), F32, 16640)]
        stab = [pview(poolf, "stab0", (128, 64), F32, 16896),
                pview(poolf, "stab1", (128, 64), F32, 17152)]
        tba = pview(poolf, "tba", (128, 512), F32, 17408)
        tbb = pview(poolf, "tbb", (128, 512), F32, 19456)
        routf = pview(poolf, "routf", (64, 1024), F32, 21504)
        rtm = pview(poolf, "rtm", (128, 64), F32, 25600)
        gtmp = pview(poolf, "gtmp", (128, 64), F32, 25856)
        efb = pview(poolf, "efb", (128, 64), F32, 26112)
        gdup = pview(poolf, "gdup", (128, 1024), F32, 26368)
        ktabc = [pview(poolf, "ktabc0", (128, 256), F32, 32768),
                 pview(poolf, "ktabc1", (128, 256), F32, 33792)]
        ktabs = [pview(poolf, "ktabs0", (128, 256), F32, 34816),
                 pview(poolf, "ktabs1", (128, 256), F32, 35840)]
        krraw = pview(poolf, "krraw", (64, 256), F32, 36864)
        krf32 = pview(poolf, "krf32", (128, 128), F32, 37888)
        krtA = pview(poolf, "krtA", (128, 64), F32, 38400)
        krtB = pview(poolf, "krtB", (128, 64), F32, 38656)
        accb2 = [pview(poolf, "accb", (128, 512), F32, 0),
                 pview(poolf, "accb1", (128, 512), F32, 2048)]
        olatb2 = [None, None]
        olatT2 = [None, None]
        olatb2[1] = pview(poolf, "olatb1", (128, 512), BF16, 4096)
        olatT2[1] = pview(poolf, "olatT1", (128, 512), BF16, 5120)
        probsT_extra = [pview(poolf, "probsT2", (128, 1024), BF16, 6144),
                        pview(poolf, "probsT3", (128, 1024), BF16, 8192)]
        accb_x = [pview(poolf, "accb2b", (128, 512), F32, 10240),
                  pview(poolf, "accb3b", (128, 512), F32, 12288)]
        olatb_x = [pview(poolf, "olatbX", (128, 512), BF16, 14336),
                   pview(poolf, "olatbY", (128, 512), BF16, 15360)]
        olatT_x = [pview(poolf, "olatTX", (128, 512), BF16, 16384),
                   pview(poolf, "olatTY", (128, 512), BF16, 17408)]
        qlat1 = pview(poolf, "qlat1", (128, 8192), BF16, 18432)
        probsb_x = [pview(poolf, "probsb2", (128, 1024), BF16, 30720),
                    pview(poolf, "probsb3", (128, 1024), BF16, 32768)]
        wukb_x = [pview(poolf, "wukb2", (128, 512), BF16, 26624),
                  pview(poolf, "wukb3", (128, 512), BF16, 27648)]
        wuvb_x = [pview(poolf, "wuvb2", (128, 512), BF16, 28672),
                  pview(poolf, "wuvb3", (128, 512), BF16, 29696)]
        dsb = [pview(poolf, "dsb0", (128, 512), F32, 0),
               pview(poolf, "dsb1", (128, 512), F32, 2048),
               pview(poolf, "dsb2", (128, 512), F32, 12288),
               pview(poolf, "dsb3", (128, 512), F32, 14336)]
        xdb = [pview(poolf, "xdb0", (128, 512), F32, 4096),
               pview(poolf, "xdb1", (128, 512), F32, 6144),
               pview(poolf, "xdb2", (128, 512), F32, 16384),
               pview(poolf, "xdb3", (128, 512), F32, 18432)]
        outbf = [pview(poolf, "outb0", (128, 512), F32, 8192),
                 pview(poolf, "outb1", (128, 512), F32, 10240)]

        ps = [es.enter_context(nc.psum_tensor(f"ps{i}", [128, 1024], F32))
              for i in range(4)]

        def bank(i):
            return ps[i // 2][:, (i % 2) * 512:(i % 2) * 512 + 512]

        def bankb(i):
            return ps[i // 2][:].bitcast(BF16)[:, (i % 2) * 1024:(i % 2) * 1024 + 1024]

        def pair(i):
            return ps[i][:]

        def t_run(fns):
            for f in fns[:-1]:
                kb.raw("tensor", f)
            return kb.t(fns[-1])

        uses = {}

        def use(nm):
            uses[nm] = uses.get(nm, 0) + 1
            return uses[nm]

        def dma(eng, sem_name, out_ap, in_ap):
            kb.op(eng, lambda e, o=out_ap, i=in_ap: e.dma_start(out=o, in_=i),
                  inc=(sem_name, 16))
            return 16 * use(sem_name)

        dma("sync", "ya", idb[:], C_IDB[:])
        dma("sync", "ya", idf[:], C_IDF[:])
        dma("sync", "ya", mU[:], C_MU[:])
        consts_done = dma("sync", "ya", mV[:], C_MV[:])
        for eng in KB.ENGS:
            kb.wait(eng, "ya", consts_done)
        kb.v(lambda e: e.memset(ST["zero"], 0.0))
        kb.drain("vector")

        lnfree = {("V", 0): 0, ("V", 1): 0, ("S", 0): 0, ("S", 1): 0,
                  ("V", 2): 0, ("S", 2): 0}

        def R(nm, slot):
            return ST[nm + ("Z" if slot else "")]

        def ln_load(xsrc_row_ap, bslot):
            sx = f"yx{bslot}"
            kb.wait("sync", "V", lnfree[("V", bslot)])
            kb.wait("sync", "S", lnfree[("S", bslot)])
            dma("sync", sx, xbuf[bslot][:, 0:2048], xsrc_row_ap[:, 0:2048])
            return dma("sync", sx, xbuf[bslot][:, 2048:4096],
                       xsrc_row_ap[:, 2048:4096])

        def emit_ln_stats_a(xsrc_row_ap, slot, preloaded=None, bslot=None):
            if bslot is None:
                bslot = slot
            sx = f"yx{bslot}"
            if preloaded is None:
                nl = ln_load(xsrc_row_ap, bslot)
            else:
                nl = preloaded
            xb = xbuf[bslot][:]
            kb.wait("vector", sx, nl)
            kb.drain("vector")
            kb.raw("vector", lambda e, xb=xb, slot=slot:
                   e.reduce_sum(R("xsum", slot), xb[:, 0:2048], AX))
            kb.wait("scalar", sx, nl)
            kb.drain("scalar")
            kb.raw("scalar", lambda e, xb=xb, slot=slot: e.activation(
                sqd[:], xb[:, 0:2048], ACT.Square, accum_out=R("xsqa", slot)))
            kb.drain("scalar")
            kb.raw("scalar", lambda e, xb=xb, slot=slot: e.activation(
                sqd[:], xb[:, 2048:4096], ACT.Square, accum_out=R("xsqb", slot)))
            kb.drain("scalar")
            s_sq = kb.s(lambda e, xb=xb, slot=slot: e.activation(
                sqd[:, 0:2048], xb[:, 2048:4096], ACT.Identity,
                accum_out=R("xsum2", slot)))
            lnfree[("S", bslot)] = s_sq
            lnpend[slot] = (xb, s_sq, bslot)

        def emit_ln_stats_b(slot):
            xb, s_sq, bslot = lnpend[slot]
            kb.wait("vector", "S", s_sq)
            kb.drain("vector")
            kb.raw("vector", lambda e, slot=slot: e.tensor_add(
                R("xsum", slot), R("xsum", slot), R("xsum2", slot)))
            kb.drain("vector")
            kb.raw("vector", lambda e, slot=slot: e.tensor_scalar(
                R("negm", slot), R("xsum", slot), -1.0 / 4096.0, None, ALU.mult))
            kb.raw("vector", lambda e, slot=slot: e.tensor_add(
                R("ex2", slot), R("xsqa", slot), R("xsqb", slot)))
            kb.drain("vector")
            kb.raw("vector", lambda e, slot=slot: e.tensor_mul(
                R("m2", slot), R("negm", slot), R("negm", slot)))
            kb.raw("vector", lambda e, slot=slot: e.tensor_scalar(
                R("ex2", slot), R("ex2", slot), 1.0 / 4096.0, None, ALU.mult))
            kb.drain("vector")
            v_stats = kb.v(lambda e, slot=slot: e.scalar_tensor_tensor(
                R("varv", slot), R("ex2", slot), EPS, R("m2", slot),
                ALU.add, ALU.subtract))
            kb.wait("scalar", "V", v_stats)
            kb.drain("scalar")
            kb.raw("scalar", lambda e, slot=slot: e.activation(
                R("lnv", slot), R("varv", slot), ACT.Ln))
            kb.drain("scalar")
            s_rstd = kb.s(lambda e, slot=slot: e.activation(
                R("rstd", slot), R("lnv", slot), ACT.Exp, scale=-0.5))
            kb.wait("vector", "S", s_rstd)
            kb.drain("vector")
            kb.raw("vector", lambda e, slot=slot: e.tensor_mul(
                R("nmr", slot), R("negm", slot), R("rstd", slot)))
            kb.drain("vector")
            tguard = lnfree.get(("T", slot), 0)
            kb.wait("vector", "T", tguard)
            v_h1 = kb.v(lambda e, xb=xb, slot=slot: e.tensor_scalar(
                xn_tm[slot][:, 0:2048], xb[:, 0:2048], R("rstd", slot),
                R("nmr", slot), ALU.mult, ALU.add))
            v_xn = kb.v(lambda e, xb=xb, slot=slot: e.tensor_scalar(
                xn_tm[slot][:, 2048:4096], xb[:, 2048:4096], R("rstd", slot),
                R("nmr", slot), ALU.mult, ALU.add))
            lnfree[("V", bslot)] = v_xn
            return (v_h1, v_xn)

        def emit_ln_trans(slot, v_xn, slab3, colbase, tr_banks, slab_guard=0):
            tm = None
            for gi in range(4):
                bk = tr_banks[gi % len(tr_banks)]
                kb.wait("tensor", "V", v_xn[0] if gi < 2 else v_xn[1])
                kb.guard(bk)
                fns = []
                for k in range(8):
                    dc = gi * 8 + k
                    fns.append(lambda e, o=bankb(bk)[:, k * 128:(k + 1) * 128],
                               i=xn_tm[slot][:, dc * 128:(dc + 1) * 128]:
                               e.transpose(o, i, idb[:]))
                tm = t_run(fns)
                eng = "vector" if gi % 2 == 0 else "scalar"
                enm = "V" if eng == "vector" else "S"
                kb.wait(eng, "T", tm)
                kb.wait(eng, "T", slab_guard)
                dst = slab3[:, gi * 8:(gi + 1) * 8, colbase:colbase + 128]
                src = view3(bankb(bk), 128)[:, 0:8, :]
                mv = kb.op(eng, lambda e, d=dst, s=src: CP(e, d, s),
                           inc=(enm, 1))
                kb.release(bk, enm, mv)
            lnfree[("T", slot)] = tm

        lnpend = {}

        def stage_barrier():
            snap = dict(kb.cnt)
            for eng in KB.ENGS:
                for nm, v in snap.items():
                    if nm in ("yx0", "yx1", "yx2"):
                        continue
                    kb.wait(eng, nm, v)

        xnfmB3 = view3(xnfmB[:], 1024)
        cq3 = view3(cqfm[:], 1024)
        lnevac = {}
        nlq = {0: ln_load(XOWN[0:128, :], 0)}
        for ch in range(8):
            if ch + 1 < 8:
                nlq[ch + 1] = ln_load(XOWN[(ch + 1) * 128:(ch + 2) * 128, :],
                                      (ch + 1) % 3)
            if ch > 0:
                vp = emit_ln_stats_b((ch - 1) % 2)
                emit_ln_trans((ch - 1) % 2, vp, xnfmB3, (ch - 1) * 128,
                              (4, 5, 6, 7))
                lnevac[ch - 1] = (kb.cnt["V"], kb.cnt["S"])
            emit_ln_stats_a(XOWN[ch * 128:(ch + 1) * 128, :], ch % 2,
                            preloaded=nlq[ch], bslot=ch % 3)
        vp = emit_ln_stats_b(1)
        emit_ln_trans(1, vp, xnfmB3, 7 * 128, (4, 5, 6, 7))
        lnevac[7] = (kb.cnt["V"], kb.cnt["S"])

        kb.wait("tensor", "V", kb.cnt["V"])
        kb.wait("tensor", "S", kb.cnt["S"])

        wfree = {0: 0, 1: 0}

        def b_proj(widx, n_out_groups, rhs3, nrhs_chunks, out_pair, lhsT_view_fn,
                   src_fn, evac_fn, tag, after_load=None, oi0=0):
            for oi in range(oi0, oi0 + n_out_groups):
                slot = oi % 2
                g = f"g{slot}"
                kb.wait("gpsimd", "T", wfree[slot])
                dma("gpsimd", g, *src_fn(oi, slot))
                if after_load:
                    after_load(oi)
                pp = oi % 2
                kb.wait("tensor", g, 16 * uses[g])
                kb.guard(2 * pp, 2 * pp + 1)
                fns = []
                for dc in range(nrhs_chunks):
                    for th in range(2):
                        fns.append(lambda e, o=pair(pp)[:, th * 512:(th + 1) * 512],
                                   l=lhsT_view_fn(oi, slot, dc),
                                   r=rhs3[:, dc, th * 512:(th + 1) * 512],
                                   st=(dc == 0), sp=(dc == nrhs_chunks - 1):
                                   e.matmul(o, l, r, start=st, stop=sp))
                tm = t_run(fns)
                wfree[slot] = tm
                evac_fn(oi, pp, tm)

        def cq_src(oi, slot):
            return (wbuf[slot][:], WQD[oi])

        def cq_lhs(oi, slot, dc):
            return view3(wbuf[slot][:], 128)[:, dc, :]

        def cq_evac(oi, pp, tm):
            eng = "vector" if oi % 2 == 0 else "scalar"
            enm = "V" if eng == "vector" else "S"
            kb.wait(eng, "T", tm)
            mv = kb.op(eng, lambda e, d=cq3[:, oi, :], s=pair(pp):
                       CP(e, d, s), inc=(enm, 1))
            kb.release(2 * pp, enm, mv)
            kb.release(2 * pp + 1, enm, mv)

        for slot in (0, 1):
            dma("gpsimd", f"g{slot}", wbuf[slot][:], WQD[slot])
        kb.guard(0, 1, 2, 3)
        tmg = {}
        for step in range(10):
            for oi in (0, 1):
                ch = step - 2 * oi
                if not (0 <= ch <= 7):
                    continue
                kb.wait("tensor", "V", lnevac[ch][0])
                kb.wait("tensor", "S", lnevac[ch][1])
                if ch == 0:
                    kb.wait("tensor", f"g{oi}", 16 * uses[f"g{oi}"])
                fns = []
                for k in range(4):
                    dc = ch * 4 + k
                    for th in range(2):
                        fns.append(lambda e, o=pair(oi)[:, th * 512:(th + 1) * 512],
                                   l=view3(wbuf[oi][:], 128)[:, dc, :],
                                   r=xnfmB3[:, dc, th * 512:(th + 1) * 512],
                                   st=(dc == 0), sp=(dc == 31):
                                   e.matmul(o, l, r, start=st, stop=sp))
                if ch == 7:
                    tmg[oi] = t_run(fns)
                else:
                    for f in fns:
                        kb.raw("tensor", f)
        cq_evac(0, 0, tmg[0])
        cq_evac(1, 1, tmg[1])
        wfree[0] = tmg[0]
        wfree[1] = tmg[1]
        b_proj(0, 10, xnfmB3, 32, None, cq_lhs, cq_src, cq_evac, "cq", oi0=2)

        slot = 0
        kb.wait("gpsimd", "T", wfree[slot])
        dma("gpsimd", "g0", wbuf[0][:, 0:2048], WRT[:])
        kb.wait("tensor", "g0", 16 * uses["g0"])
        kb.guard(0, 1)
        fns = []
        for dc in range(32):
            for th in range(2):
                fns.append(lambda e, o=pair(0)[0:64, th * 512:(th + 1) * 512],
                           l=view3(wbuf[0][:], 64)[:, dc, :],
                           r=xnfmB3[:, dc, th * 512:(th + 1) * 512],
                           st=(dc == 0), sp=(dc == 31):
                           e.matmul(o, l, r, start=st, stop=sp))
        tm = t_run(fns)
        wfree[0] = tm
        kb.wait("vector", "T", tm)
        v_rout = kb.v(lambda e: e.tensor_copy(routf[:], pair(0)[0:64, :]))
        kb.release(0, "V", v_rout)
        kb.release(1, "V", v_rout)

        v_gprev = v_rout
        for tc in range(8):
            kb.wait("tensor", "V", v_gprev)
            kb.guard(4)
            t_rt = kb.t(lambda e, i=routf[0:64, tc * 128:(tc + 1) * 128]:
                        e.transpose(bank(4)[:, 0:64], i, idf[0:64, 0:64]))
            kb.wait("vector", "T", t_rt)
            v_rtm = kb.v(lambda e: e.tensor_copy(rtm[:], bank(4)[:, 0:64]))
            kb.release(4, "V", v_rtm)
            kb.drain("vector")
            kb.raw("vector", lambda e: e.max(top16[:, 0:8], rtm[:]))
            kb.drain("vector")
            kb.raw("vector", lambda e: e.match_replace(gtmp[:], top16[:, 0:8], rtm[:], -3e38))
            kb.drain("vector")
            kb.raw("vector", lambda e: e.max(top16[:, 8:16], gtmp[:]))
            kb.drain("vector")
            v_t16 = kb.v(lambda e: e.tensor_scalar(
                ST["negm16"], top16[:, 0:1], -1.0, None, ALU.mult))
            kb.wait("scalar", "V", v_t16)
            kb.raw("scalar", lambda e: e.activation(
                top16e[:], top16[:], ACT.Exp, bias=ST["negm16"], accum_out=ST["l16"]))
            kb.drain("scalar")
            s_ef = kb.s(lambda e: e.activation(
                efb[:], rtm[:], ACT.Exp, bias=ST["negm16"]))
            kb.wait("vector", "S", s_ef)
            kb.raw("vector", lambda e: e.reciprocal(ST["linv16"], ST["l16"]))
            kb.raw("vector", lambda e: e.tensor_scalar(
                gtmp[:], rtm[:], top16[:, 15:16], None, ALU.is_ge))
            kb.drain("vector")
            kb.raw("vector", lambda e: e.tensor_scalar(
                ST["scl2"], ST["linv16"], 2.0, None, ALU.mult))
            kb.raw("vector", lambda e: e.tensor_mul(rtm[:], efb[:], gtmp[:]))
            kb.drain("vector")
            v_g = kb.v(lambda e: e.tensor_scalar(
                rtm[:], rtm[:], ST["scl2"], None, ALU.mult))
            kb.wait("tensor", "V", v_g)
            kb.guard(5)
            t_g = kb.t(lambda e: e.transpose(bank(5)[0:64, 0:128], rtm[:], idf[:]))
            kb.wait("vector", "T", t_g)
            v_gprev = kb.v(lambda e, d=gdup[0:64, tc * 128:(tc + 1) * 128]:
                           e.tensor_copy(d, bank(5)[0:64, 0:128]))
            kb.release(5, "V", v_gprev)
        kb.wait("sync", "V", v_gprev)
        gdup_done = dma("sync", "ya", gdup[64:128, :], gdup[0:64, :])

        def h_src(oi, slot):
            return (wbuf[slot][:], A2[oi])

        def h_evac(oi, pp, tm):
            slot = oi % 2
            kb.wait("vector", "T", tm)
            kb.wait("vector", "ya", gdup_done)
            kb.wait("vector", "ysb", 16 * uses.get("ysb", 0))
            v_hg = kb.v(lambda e, d=qn_stg[slot][:], s=pair(pp):
                        e.tensor_tensor(d, s, gdup[:], ALU.mult))
            kb.release(2 * pp, "V", v_hg)
            kb.release(2 * pp + 1, "V", v_hg)
            kb.wait("sync", "V", v_hg)
            dma("sync", "ysb", HGS[:, oi, :], qn_stg[slot][:])

        b_proj(0, 8, xnfmB3, 32, None, cq_lhs, h_src, h_evac, "h")
        h_tm = kb.cnt["T"]

        _gclast = [0]

        def cast(dst, srcw):
            if _gclast[0]:
                kb.wait("gpsimd", "gc", _gclast[0])
            _gclast[0] = dma("gpsimd", "gc", dst, srcw)

        def wqr_hook(oi):
            if oi % 4 == 3 and oi // 4 < 8:
                j = oi // 4
                if j == 0:
                    kb.wait("gpsimd", "T", h_tm)
                dma("gpsimd", "gq",
                    wqrb[:, j * 3072:(j + 1) * 3072],
                    WQR[j // 4][:, (j % 4) * 3072:(j % 4 + 1) * 3072])
                if j == 7:
                    cast(WKVCC[:], WKVC[:].rearrange("p c j -> p (c j)"))

        def qn_src(oi, slot):
            return (wbuf[slot][:, 0:1536], WQN[oi])

        def qn_evac(oi, pp, tm):
            slot = oi % 2
            eng = "vector" if oi % 2 == 0 else "scalar"
            enm = "V" if eng == "vector" else "S"
            kb.wait(eng, "T", tm)
            kb.wait(eng, "ysb", 16 * uses.get("ysb", 0))
            mv = kb.op(eng, lambda e, d=qn_stg[slot][:], s=pair(pp):
                       CP(e, d, s), inc=(enm, 1))
            kb.release(2 * pp, enm, mv)
            kb.release(2 * pp + 1, enm, mv)
            kb.wait("sync", enm, mv)
            dma("sync", "ysb", QNS[:, oi, :], qn_stg[slot][:])

        b_proj(0, 32, cq3, 12, None, cq_lhs, qn_src, qn_evac, "qn",
               after_load=wqr_hook)
        wqr_loaded = 16 * uses["gq"]

        qro_free = {0: 0, 1: 0}
        for tc in range(8):
            tsl = tc % 2
            kb.wait("sync", "V", qro_free[tsl])
            dma("sync", f"yt{tsl}", ctab[tsl][:], COSQ[tc * 128:(tc + 1) * 128, :])
            tab_n = dma("sync", f"yt{tsl}", stab[tsl][:], SINQ[tc * 128:(tc + 1) * 128, :])
            kb.wait("tensor", "gq", wqr_loaded)
            kb.wait("tensor", "V", qro_free[tsl])
            v_drt = {}
            for drt in range(4):
                kb.guard(drt)
                fns = []
                for qc in range(12):
                    fns.append(lambda e, o=bank(drt), drt=drt,
                               l=cq3[:, qc, tc * 128:(tc + 1) * 128],
                               r=view3(wqrb[:], 2048)[:, qc, drt * 512:(drt + 1) * 512],
                               st=(qc == 0), sp=(qc == 11):
                               e.matmul(o, l, r, start=st, stop=sp))
                tm_d = t_run(fns)
                pb3 = view3(bank(drt), 64)
                ct = ctab[tsl][:].unsqueeze(1).broadcast_to([128, 8, 64])
                sl = stab[tsl][:, 0:32].unsqueeze(1).broadcast_to([128, 8, 32])
                sh = stab[tsl][:, 32:64].unsqueeze(1).broadcast_to([128, 8, 32])
                kb.wait("vector", "T", tm_d)
                if drt == 0:
                    kb.wait("vector", f"yt{tsl}", tab_n)
                kb.drain("vector")
                kb.raw("vector", lambda e, pb3=pb3, ct=ct:
                       e.tensor_tensor(view3(tba[:], 64), pb3, ct, ALU.mult))
                kb.raw("vector", lambda e, pb3=pb3, sl=sl:
                       e.tensor_tensor(view3(tbb[:], 64)[:, :, 0:32],
                                       pb3[:, :, 32:64], sl, ALU.mult))
                kb.raw("vector", lambda e, pb3=pb3, sh=sh:
                       e.tensor_tensor(view3(tbb[:], 64)[:, :, 32:64],
                                       pb3[:, :, 0:32], sh, ALU.mult))
                kb.drain("vector")
                v_d = kb.v(lambda e, drt=drt, tsl=tsl:
                           e.tensor_add(qro_tm[tsl][:, drt * 512:(drt + 1) * 512],
                                        tba[:], tbb[:]))
                v_drt[drt] = v_d
                kb.release(drt, "V", v_d)
            v_last = v_drt[3]
            qro_free[tsl] = v_last
            for gi in range(4):
                bk = 4 + gi % 2
                kb.wait("tensor", "V", v_drt[gi])
                kb.guard(bk)
                fns = []
                for k in range(4):
                    fc = gi * 4 + k
                    fns.append(lambda e, o=bankb(bk)[:, k * 128:(k + 1) * 128],
                               i=qro_tm[tsl][:, fc * 128:(fc + 1) * 128]:
                               e.transpose(o, i, idb[:]))
                tmg = t_run(fns)
                sslot = gi % 2
                eng = "vector" if gi % 2 == 0 else "scalar"
                enm = "V" if eng == "vector" else "S"
                kb.wait(eng, "T", tmg)
                kb.wait(eng, "ysb", 16 * uses.get("ysb", 0))
                mv = kb.op(eng, lambda e, d=tr_stg[sslot][:], s=bankb(bk)[:, 0:512]:
                           CP(e, d, s), inc=(enm, 1))
                kb.release(bk, enm, mv)
                kb.wait("sync", enm, mv)
                dst = QRS[:, gi * 4:(gi + 1) * 4, tc * 128:(tc + 1) * 128]
                dma("sync", "ysb", dst, view3(tr_stg[sslot][:], 128))

        b_ysb = 16 * uses.get("ysb", 0)

        kb.wait("sync", "V", lnfree[("V", 0)])
        kb.wait("sync", "S", lnfree[("S", 0)])
        dma("sync", "yx0", xbuf[0][:, 0:2048], XKV[0:128, 0:2048])
        a_pre_nl = dma("sync", "yx0", xbuf[0][:, 2048:4096], XKV[0:128, 2048:4096])
        kb.wait("sync", "V", lnfree[("V", 2)])
        kb.wait("sync", "S", lnfree[("S", 2)])
        dma("sync", "yx2", xbuf[2][:, 0:2048], XKV[128:256, 0:2048])
        a_pre_nl2 = dma("sync", "yx2", xbuf[2][:, 2048:4096], XKV[128:256, 2048:4096])

        stage_barrier()

        slab3P = [view3(slabA2[0][:], 256), view3(slabA2[1][:], 256)]
        ckvfm3 = view3(ckvfm[:], 4096)
        kb.wait("sync", "gc", 16)
        dma("sync", "ya", wkvcb[:, 0:9216], WKVCC[:, 0:9216])
        wkv_loaded = dma("sync", "ya", wkvcb[:, 9216:18432], WKVCC[:, 9216:18432])

        lnfree[("V", 0)] = kb.cnt["V"]
        lnfree[("V", 1)] = kb.cnt["V"]
        lnfree[("S", 0)] = kb.cnt["S"]
        lnfree[("S", 1)] = kb.cnt["S"]
        ktab_free = {0: None, 1: None}
        ktab_ns = {}
        kv_done = {}
        ln_done = {}

        def a_ln_stats_tile(t):
            for i4 in range(2):
                ch = t * 2 + i4
                if ch == 0:
                    emit_ln_stats_a(XKV[0:128, :], 0, preloaded=a_pre_nl)
                elif ch == 1:
                    emit_ln_stats_a(XKV[128:256, :], 1, preloaded=a_pre_nl2,
                                    bslot=2)
                else:
                    emit_ln_stats_a(XKV[ch * 128:(ch + 1) * 128, :], ch % 2)
            return [emit_ln_stats_b((t * 2) % 2), emit_ln_stats_b((t * 2 + 1) % 2)]

        def a_ln_trans_tile(t, vxs):
            slot = t % 2
            for i4 in range(2):
                ch = t * 2 + i4
                emit_ln_trans(ch % 2, vxs[i4], slab3P[slot], i4 * 128, (6, 7),
                              slab_guard=kv_done.get(t - 2, 0))
            ln_done[t] = (kb.cnt["V"], kb.cnt["S"])

        def a_kv_tile(t):
            slot = t % 2
            slab3 = slab3P[slot]
            tsl = t % 2
            kf = ktab_free[tsl]
            if kf:
                kb.wait("sync", kf[0], kf[1])
            dma("sync", f"yt{tsl}", view3(ktabc[tsl][:, 0:128], 64),
                COSK[t * 256:(t + 1) * 256, :].rearrange("(i p) j -> p i j", p=128))
            ktab_ns[t] = dma("sync", f"yt{tsl}", view3(ktabs[tsl][:, 0:128], 64),
                             SINK[t * 256:(t + 1) * 256, :].rearrange("(i p) j -> p i j", p=128))
            kb.wait("tensor", "ya", wkv_loaded)
            lv, ls = ln_done[t]
            kb.wait("tensor", "V", lv)
            kb.wait("tensor", "S", ls)
            kb.guard(0, 1, 2, 3, 4)
            fns = []
            for dc in range(32):
                for rc in range(4):
                    fns.append(lambda e, o=bank(rc)[:, 0:256],
                               l=view3(wkvcb[:], 576)[:, dc, rc * 128:(rc + 1) * 128],
                               r=slab3[:, dc, :], st=(dc == 0), sp=(dc == 31):
                               e.matmul(o, l, r, start=st, stop=sp))
                fns.append(lambda e, o=bank(4)[0:64, 0:256],
                           l=view3(wkvcb[:], 576)[:, dc, 512:576],
                           r=slab3[:, dc, :], st=(dc == 0), sp=(dc == 31):
                           e.matmul(o, l, r, start=st, stop=sp))
            tm = t_run(fns)
            kv_done[t] = tm
            return tm

        post_state = {}
        last_tkb = {}

        def a_kv_post1(t):
            tm = kv_done[t]
            slab3 = slab3P[t % 2]
            tsl = t % 2
            ktab_n = ktab_ns[t]
            kb.wait("vector", "T", tm)
            v_e0 = kb.v(lambda e, t=t: e.tensor_copy(
                ckvfm3[:, 0:2, t * 256:(t + 1) * 256],
                view3(pair(0), 512)[:, :, 0:256]))
            kb.release(0, "V", v_e0)
            kb.release(1, "V", v_e0)
            kb.wait("scalar", "T", tm)
            s_e1 = kb.s(lambda e, t=t: CP(e,
                ckvfm3[:, 2:4, t * 256:(t + 1) * 256],
                view3(pair(1), 512)[:, :, 0:256]))
            kb.release(2, "S", s_e1)
            kb.release(3, "S", s_e1)
            kb.wait("scalar", "T", tm)
            s_kr = kb.s(lambda e: CP(e, krraw[0:64, 0:256], bank(4)[0:64, 0:256]))
            kb.release(4, "S", s_kr)
            kb.wait("tensor", "S", s_kr)
            kb.guard(5)
            fns = []
            for i4 in range(2):
                fns.append(lambda e, o=bank(5)[:, i4 * 64:(i4 + 1) * 64],
                           i=krraw[0:64, i4 * 128:(i4 + 1) * 128]:
                           e.transpose(o, i, idf[0:64, 0:64]))
            t_kt = t_run(fns)
            kb.wait("vector", "T", t_kt)
            v_krf = kb.v(lambda e: e.tensor_copy(krf32[:], bank(5)[:, 0:128]))
            kb.release(5, "V", v_krf)
            kb.wait("vector", f"yt{tsl}", ktab_n)
            kb.wait("vector", "T", last_tkb.get(tsl, 0))
            p_kro = None
            for i4 in range(2):
                pbk = krf32[:, i4 * 64:(i4 + 1) * 64]
                ct = view3(ktabc[tsl][:, 0:128], 64)[:, i4, :]
                slo = view3(ktabs[tsl][:, 0:128], 64)[:, i4, 0:32]
                shi = view3(ktabs[tsl][:, 0:128], 64)[:, i4, 32:64]
                kb.drain("vector")
                kb.raw("vector", lambda e, pbk=pbk, ct=ct:
                       e.tensor_tensor(krtA[:], pbk, ct, ALU.mult))
                kb.raw("vector", lambda e, pbk=pbk, slo=slo:
                       e.tensor_tensor(krtB[:, 0:32], pbk[:, 32:64], slo, ALU.mult))
                kb.raw("vector", lambda e, pbk=pbk, shi=shi:
                       e.tensor_tensor(krtB[:, 32:64], pbk[:, 0:32], shi, ALU.mult))
                kb.drain("vector")
                p_kro = kb.v(lambda e, i4=i4, tsl=tsl: e.tensor_add(
                    krtm2[tsl][:, i4 * 64:(i4 + 1) * 64], krtA[:], krtB[:]))
            ktab_free[tsl] = ("V", p_kro)
            post_state[t] = (p_kro, v_e0, s_e1)

        def a_kv_post2(t):
            p_kro, v_e0, s_e1 = post_state.pop(t)
            tsl = t % 2
            kb.wait("tensor", "V", p_kro)
            kb.guard(5)
            fns = []
            for i4 in range(2):
                fns.append(lambda e, o=bankb(5)[0:64, i4 * 128:(i4 + 1) * 128],
                           i=krtm2[tsl][:, i4 * 64:(i4 + 1) * 64]:
                           e.transpose(o, i, idb[:]))
            t_kb = t_run(fns)
            last_tkb[tsl] = t_kb
            kb.wait("scalar", "T", t_kb)
            s_krf = kb.s(lambda e, t=t: CP(e,
                krdup[0:64, t * 256:(t + 1) * 256], bankb(5)[0:64, 0:256]))
            kb.wait("tensor", "S", s_krf)
            fns = []
            for k in range(8):
                i4, rc = k // 4, k % 4
                fns.append(lambda e, o=bankb(5)[:, k * 128:(k + 1) * 128],
                           i=ckvfm3[:, rc, t * 256 + i4 * 128: t * 256 + (i4 + 1) * 128]:
                           e.transpose(o, i, idb[:]))
            tmg = t_run(fns)
            sslot = t % 2
            kb.wait("vector", "T", tmg)
            kb.wait("vector", "ysa", 16 * uses.get("ysa", 0))
            mv = kb.v(lambda e, d=ktm_stg[sslot][:, 0:1024], s=bankb(5):
                      e.tensor_copy(d, s))
            kb.release(5, "V", mv)
            kb.wait("sync", "V", mv)
            dst = KTS[:, t * 2: t * 2 + 2, :]
            dma("sync", "ysa", dst, view3(ktm_stg[sslot][:, 0:1024], 512))

        pend_vx = {}
        for t in range(18):
            if t < 16:
                pend_vx[t] = a_ln_stats_tile(t)
            if 1 <= t <= 16:
                a_kv_tile(t - 1)
            if 2 <= t <= 17:
                a_kv_post2(t - 2)
            if t < 16:
                a_ln_trans_tile(t, pend_vx.pop(t))
            if 1 <= t <= 16:
                a_kv_post1(t - 1)
        s_all = kb.cnt["S"]
        kb.wait("sync", "S", s_all)
        krdup_done = dma("sync", "ya", krdup[64:128, :], krdup[0:64, :])
        a_ysa = 16 * uses.get("ysa", 0)
        stage_barrier()

        ckvtm3 = view3(ckvtm[:], 512)
        qrfm3 = view3(qropefm[:], 1024)
        qlat3 = view3(qlat[:], 1024)
        kb.wait("sync", "ysa", a_ysa)
        kb.wait("sync", "ysb", b_ysb)
        qr_n = {}
        qr_seen = set()
        for hp in range(2):
            qr_n[hp] = dma("sync", f"yc{hp}", qrfm3[:, hp, :], QRS[:, hp, :])
        c_pre = None
        for ki in range(4):
            c_pre = dma("sync", "ya", ckvtm3[:, ki * 8:(ki + 1) * 8, :],
                        KTS[:, ki * 8:(ki + 1) * 8, :])
        c_pre_pend = [c_pre]
        kb.wait("tensor", "ya", krdup_done)

        olatb2[0] = olatb
        olatT2[0] = olatT
        olatb4 = [olatb2[0], olatb2[1], olatb_x[0], olatb_x[1]]
        olatT4 = [olatT2[0], olatT2[1], olatT_x[0], olatT_x[1]]
        accb4 = [accb2[0], accb2[1], accb_x[0], accb_x[1]]
        probsT[2] = probsT_extra[0]
        probsT[3] = probsT_extra[1]
        probsb[2] = probsb_x[0]
        probsb[3] = probsb_x[1]
        laccP = [ST["lacc"], ST["lacc1"], ST["lacc2"], ST["lacc3"]]
        linvP = [ST["linv"], ST["linv1"], ST["linv2"], ST["linv3"]]
        nm4 = [ST["negmA"], ST["negmB"], ST["negmC"], ST["negmD"]]
        fd4 = [ST["fdA"], ST["fdB"], ST["fdC"], ST["fdD"]]
        fP4 = [ST["fA"], ST["fB"], ST["fC"], ST["fD"]]
        lgP4 = [ST["lgA"], ST["lgB"], ST["lgC"], ST["lgD"]]
        qlatB = [qlat, qlat1]
        qlat3P = [view3(qlat[:], 1024), view3(qlat1[:], 1024)]
        wukb4 = [wukb[0], wukb[1], wukb_x[0], wukb_x[1]]
        wuvb4 = [wuvb[0], wuvb[1], wuvb_x[0], wuvb_x[1]]

        qn_free = {0: 0, 1: 0}
        wk_free = {0: 0, 1: 0}
        sc_free = {0: None, 1: None}
        pv_free = None
        tr_free = None
        oh_free = None
        olat_free = {}
        olT_free = {}
        tr7_free = None
        qlat_free = None

        units_base = []
        for qt in reversed(range(8)):
            d = qt + 1
            ng = (d + 1) // 2
            for g in range(ng):
                units_base.append((qt, g, ng))
        NU1 = len(units_base)

        load_done = {}

        wk_free4 = {0: 0, 1: 0, 2: 0, 3: 0}

        def issue_loads(h):
            hs = h % 2
            h4 = h % 4
            kb.wait("sync", "T", qn_free[hs])
            nq = dma("sync", f"yq{hs}", qnopeb[hs][:], QNS[:, h, :])
            kb.wait("gpsimd", "T", wk_free4[h4])
            dma("gpsimd", f"gk{h4}", wukb4[h4][:], WUK[h * 128:(h + 1) * 128, :])
            nk = dma("gpsimd", f"gk{h4}", wuvb4[h4][:], WUVT[h])
            load_done[h] = (nq, nk)
            if h % 2 == 0 and h // 2 + 2 < 16:
                p2 = h // 2 + 2
                qr_n[p2] = dma("sync", f"yc{p2 % 8}", qrfm3[:, p2, :],
                               QRS[:, p2, :])
            if h < 16:
                cast(WOUTC[h], WOUT[h])
                cast(B2C[h], B2[h])

        def emit_qlat(h, banks):
            nonlocal qlat_free
            hs = h % 2
            h4 = h % 4
            nq, nk = load_done[h]
            q3 = qlat3P[hs]
            kb.wait("tensor", f"yq{hs}", nq)
            kb.wait("tensor", f"gk{h4}", nk)
            if qlat_free:
                kb.wait("tensor", qlat_free[0], qlat_free[1])
            lastev = {}
            vm = sm = None
            for rc in range(4):
                for qh in range(2):
                    bk = banks[(rc * 2 + qh) % 4]
                    pv = lastev.get(bk)
                    if pv:
                        kb.wait("tensor", pv[0], pv[1])
                    else:
                        kb.guard(bk)
                    tmq = kb.t(lambda e, o=bank(bk), hs=hs, h4=h4,
                               l=wukb4[h4][:, rc * 128:(rc + 1) * 128],
                               r=qnopeb[hs][:, qh * 512:(qh + 1) * 512]:
                               e.matmul(o, l, r, start=True, stop=True))
                    eng = "vector" if qh == 0 else "scalar"
                    enm = "V" if qh == 0 else "S"
                    kb.wait(eng, "T", tmq)
                    mv = kb.op(eng, lambda e, d=q3[:, rc, qh * 512:(qh + 1) * 512],
                               s=bank(bk): SMUL(e, d, s, SCALE),
                               inc=(enm, 1))
                    lastev[bk] = (enm, mv)
                    kb.release(bk, enm, mv)
                    if enm == "V":
                        vm = mv
                    else:
                        sm = mv
            qlat_free = lastev[banks[3]]
            qn_free[hs] = kb.cnt["T"]
            return (vm, sm)

        def emit_scores(h, qt, g, ng, par):
            d = qt + 1
            hb = 64 * (h % 2)
            hp = h // 2
            if hp not in qr_seen:
                qr_seen.add(hp)
                kb.wait("tensor", f"yc{hp % 8}", qr_n[hp])
            q3 = qlat3P[h % 2]
            kbs = [2 * g] + ([2 * g + 1] if 2 * g + 1 < d else [])
            pp = pair(par)
            rel = sc_free[par]
            if rel:
                kb.wait("tensor", "V", rel[0])
                kb.wait("tensor", "S", rel[1])
            kb.guard(2 * par, 2 * par + 1)
            fns = []
            for rc in range(4):
                for j, kbi in enumerate(kbs):
                    fns.append(lambda e, o=pp[:, j * 512:(j + 1) * 512],
                               l=q3[:, rc, qt * 128:(qt + 1) * 128],
                               r=ckvfm3[:, rc, kbi * 512:(kbi + 1) * 512],
                               st=(rc == 0): e.matmul(o, l, r, start=st, stop=False))
            for j, kbi in enumerate(kbs):
                diag = (kbi == d - 1)
                fns.append(lambda e, o=pp[:, j * 512:(j + 1) * 512],
                           l=qrfm3[hb:hb + 64, h // 2, qt * 128:(qt + 1) * 128],
                           r=krdup[hb:hb + 64, kbi * 512:(kbi + 1) * 512],
                           sp=(not diag): e.matmul(o, l, r, start=False, stop=sp))
                if diag:
                    fns.append(lambda e, o=pp[:, j * 512:(j + 1) * 512]:
                               e.matmul(o, mU[:], mV[:], start=False, stop=True))
            return t_run(fns), len(kbs)

        def emit_tail_a_vs(qt, g, ng, par, t_sc, nkb, i4x):
            pp = pair(par)
            wid = 512 * nkb
            kb.wait("vector", "T", t_sc)
            if g == 0:
                kb.drain("vector")
                v_mx = kb.v(lambda e, b=nm4[i4x]: e.reduce_max(
                    b, pp[:, 0:wid], AX, negate=True))
                cur = nm4[i4x]
            else:
                kb.drain("vector")
                kb.raw("vector", lambda e: e.reduce_max(
                    ST["negmg"], pp[:, 0:wid], AX, negate=True))
                kb.drain("vector")
                kb.raw("vector", lambda e, a=nm4[(i4x + 2) % 4], b=nm4[i4x]:
                       e.tensor_tensor(b, a, ST["negmg"], ALU.min))
                v_mx = kb.v(lambda e, a=nm4[(i4x + 2) % 4], f=fd4[i4x]:
                            e.scalar_tensor_tensor(
                                f, ST["negmg"], a, ST["zero"],
                                ALU.subtract, ALU.min))
                cur = nm4[i4x]
            kb.wait("scalar", "V", v_mx)
            if g > 0:
                kb.raw("scalar", lambda e, f=fd4[i4x], o=fP4[i4x]:
                       e.activation(o, f, ACT.Exp))
            s_exp = kb.s(lambda e, o=probsb[i4x][:, 0:wid], i=pp[:, 0:wid],
                         b=cur, l=lgP4[i4x]:
                         e.activation(o, i, ACT.Exp, bias=b, accum_out=l))
            sc_free[par] = (v_mx, s_exp)
            return s_exp

        def emit_tail_a_t(qt, g, ng, par, s_exp, nkb, pt4):
            nonlocal tr_free
            wid = 512 * nkb
            kb.wait("tensor", "S", s_exp)
            if tr_free:
                kb.wait("tensor", tr_free[0], tr_free[1])
            fns = []
            for tcn in range(4 * nkb):
                fns.append(lambda e, o=bankb(5)[:, tcn * 128:(tcn + 1) * 128],
                           i=probsb[pt4][:, tcn * 128:(tcn + 1) * 128]:
                           e.transpose(o, i, idb[:]))
            t_tr = t_run(fns)
            eng = "vector" if par == 0 else "scalar"
            enm = "V" if par == 0 else "S"
            kb.wait(eng, "T", t_tr)
            mv = kb.op(eng, lambda e, d=probsT[pt4][:, 0:wid], s=bankb(5)[:, 0:wid]:
                       CP(e, d, s), inc=(enm, 1))
            tr_free = (enm, mv)
            return mv, enm

        def emit_tail_b(qt, g, ng, par, nkb, ev, qp, pt4):
            nonlocal pv_free
            mv, enm = ev
            kb.wait("tensor", enm, mv)
            if c_pre_pend:
                kb.wait("tensor", "ya", c_pre_pend.pop())
            if pv_free:
                kb.wait("tensor", "V", pv_free)
            fns = []
            for tcn in range(4 * nkb):
                tci = 8 * g + tcn
                fns.append(lambda e, o=bank(4),
                           l=probsT[pt4][:, tcn * 128:(tcn + 1) * 128],
                           r=ckvtm3[:, tci, :], st=(tcn == 0), sp=(tcn == 4 * nkb - 1):
                           e.matmul(o, l, r, start=st, stop=sp))
            t_pv = t_run(fns)
            kb.wait("vector", "T", t_pv)
            if g == 0:
                rel = olat_free.get(qp)
                if rel:
                    kb.wait("vector", rel[0], rel[1])
                kb.raw("vector", lambda e, qp=qp: e.tensor_copy(accb4[qp][:], bank(4)))
                v_mg = kb.v(lambda e, l=lgP4[pt4], qp=qp: e.tensor_copy(laccP[qp], l))
            else:
                kb.raw("vector", lambda e, f=fP4[pt4], qp=qp: e.scalar_tensor_tensor(
                    accb4[qp][:], accb4[qp][:], f, bank(4), ALU.mult, ALU.add))
                v_mg = kb.v(lambda e, f=fP4[pt4], l=lgP4[pt4], qp=qp: e.scalar_tensor_tensor(
                    laccP[qp], laccP[qp], f, l, ALU.mult, ALU.add))
            pv_free = v_mg
            return v_mg

        def emit_final_a_v(qt, ng, qp):
            kb.drain("vector")
            kb.raw("vector", lambda e, qp=qp: e.reciprocal(linvP[qp], laccP[qp]))
            kb.drain("vector")
            rel = olat_free.get(qp)
            if rel:
                kb.wait("vector", rel[0], rel[1])
            v_ol = kb.v(lambda e, qp=qp: e.tensor_scalar(
                olatb4[qp][:], accb4[qp][:], linvP[qp], None, ALU.mult))
            return v_ol

        def emit_final_a_t(qt, ng, qp, v_ol):
            nonlocal tr7_free
            kb.wait("tensor", "V", v_ol)
            if tr7_free:
                kb.wait("tensor", tr7_free[0], tr7_free[1])
            fns = []
            for rc in range(4):
                fns.append(lambda e, o=bankb(7)[:, rc * 128:(rc + 1) * 128],
                           i=olatb4[qp][:, rc * 128:(rc + 1) * 128]:
                           e.transpose(o, i, idb[:]))
            t_ot = t_run(fns)
            olat_free[qp] = ("T", t_ot)
            kb.wait("scalar", "T", t_ot)
            rel = olT_free.get(qp)
            if rel:
                kb.wait("scalar", rel[0], rel[1])
            s_ote = kb.s(lambda e, qp=qp: CP(e, olatT4[qp][:], bankb(7)[:, 0:512]))
            tr7_free = ("S", s_ote)
            return s_ote

        def emit_final_b(h, qt, ng, qp, s_ote):
            nonlocal oh_free
            hs = h % 2
            h4 = h % 4
            kb.wait("tensor", "S", s_ote)
            if oh_free:
                kb.wait("tensor", "S", oh_free)
            kb.guard(6)
            fns = []
            for rc in range(4):
                fns.append(lambda e, o=bank(6)[:, 0:128], h4=h4,
                           l=view3(wuvb4[h4][:], 128)[:, rc, :],
                           r=olatT4[qp][:, rc * 128:(rc + 1) * 128],
                           st=(rc == 0), sp=(rc == 3):
                           e.matmul(o, l, r, start=st, stop=sp))
            t_oh = t_run(fns)
            olT_free[qp] = ("T", t_oh)
            kb.wait("scalar", "T", t_oh)
            kb.wait("scalar", f"yat{hs}", 16 * uses.get(f"yat{hs}", 0))
            s_at = kb.s(lambda e, hs=hs, d=attnstg[hs][:, qt * 128:(qt + 1) * 128]:
                        CP(e, d, bank(6)[:, 0:128]))
            kb.release(6, "S", s_at)
            oh_free = s_at
            return s_at

        issue_loads(0)
        issue_loads(1)
        qlat_ready = {0: emit_qlat(0, [6, 7, 6, 7]), 1: emit_qlat(1, [6, 7, 6, 7])}

        def emit_qlat_part(h, rc, par):
            hs = h % 2
            h4 = h % 4
            nq, nk = load_done[h]
            q3 = qlat3P[hs]
            kb.wait("tensor", f"yq{hs}", nq)
            kb.wait("tensor", f"gk{h4}", nk)
            rel = sc_free[par]
            if rel:
                kb.wait("tensor", "V", rel[0])
                kb.wait("tensor", "S", rel[1])
            kb.guard(2 * par, 2 * par + 1)
            fns = []
            for qh in range(2):
                fns.append(lambda e, o=pair(par)[:, qh * 512:(qh + 1) * 512],
                           hs=hs, h4=h4, rc=rc,
                           l=wukb4[h4][:, rc * 128:(rc + 1) * 128],
                           r=qnopeb[hs][:, qh * 512:(qh + 1) * 512]:
                           e.matmul(o, l, r, start=True, stop=True))
            tmq = t_run(fns)
            eng = "vector" if par == 0 else "scalar"
            enm = "V" if par == 0 else "S"
            kb.wait(eng, "T", tmq)
            mv = kb.op(eng, lambda e, d=q3[:, rc, :], s=pair(par):
                       SMUL(e, d, s, SCALE), inc=(enm, 1))
            if enm == "V":
                sc_free[par] = (mv, kb.cnt["S"])
            else:
                sc_free[par] = (kb.cnt["V"], mv)
            if rc == 3:
                qn_free[hs] = kb.cnt["T"]
            return mv

        stream = []
        load_hooks = {}
        qlat_mile = {}
        for hp in range(H_ // 2):
            h0 = 2 * hp
            for k in range(NU1):
                stream.append(("U", h0) + units_base[k])
                stream.append(("U", h0 + 1) + units_base[k])
            if h0 + 2 < H_:
                load_hooks[len(stream) - 12] = h0 + 2
                load_hooks[len(stream) - 8] = h0 + 3
                for r in range(4):
                    stream.append(("Q", h0 + 2, r))
                for r in range(4):
                    stream.append(("Q", h0 + 3, r))
        NS = len(stream)

        st_sc = {}
        st_exp = {}
        st_ev = {}
        st_fv = {}
        st_fin = {}
        fin_count = {h: sum(1 for u in units_base if u[1] == u[2] - 1) for h in range(H_)}

        for hh in (0, 1):
            kb.wait("tensor", "V", qlat_ready[hh][0])
            kb.wait("tensor", "S", qlat_ready[hh][1])

        def is_final_i(i):
            e = stream[i]
            return e[0] == "U" and e[3] == e[4] - 1

        def qpx(i):
            e = stream[i]
            return (e[1] % 2) * 2 + e[2] % 2

        for tick in range(NS + 5):
            if tick in load_hooks:
                issue_loads(load_hooks[tick])
            i = tick - 1
            if 0 <= i < NS and i in st_sc:
                hh, qt, g, ng, par, t_sc, nkb = st_sc[i]
                st_exp[i] = emit_tail_a_vs(qt, g, ng, par, t_sc, nkb, i % 4)
            i = tick - 4
            if 0 <= i < NS and is_final_i(i):
                _, hh, qt, g, ng = stream[i]
                st_fv[i] = emit_final_a_v(qt, ng, qpx(i))
            if tick < NS:
                e = stream[tick]
                par = tick % 2
                if e[0] == "U":
                    _, hh, qt, g, ng = e
                    t_sc, nkb = emit_scores(hh, qt, g, ng, par)
                    st_sc[tick] = (hh, qt, g, ng, par, t_sc, nkb)
                else:
                    _, hh, rc = e
                    emit_qlat_part(hh, rc, par)
            i = tick - 3
            if 0 <= i < NS and i in st_sc:
                hh, qt, g, ng, par, t_sc, nkb = st_sc.pop(i)
                emit_tail_b(qt, g, ng, par, nkb, st_ev.pop(i), (hh % 2) * 2 + qt % 2, i % 4)
            i = tick - 2
            if 0 <= i < NS and i in st_sc:
                hh, qt, g, ng, par, t_sc, nkb = st_sc[i]
                st_ev[i] = emit_tail_a_t(qt, g, ng, par, st_exp.pop(i), nkb, i % 4)
            i = tick - 4
            if 0 <= i < NS and is_final_i(i):
                _, hh, qt, g, ng = stream[i]
                st_fin[i] = emit_final_a_t(qt, ng, qpx(i), st_fv.pop(i))
            i = tick - 5
            if 0 <= i < NS and is_final_i(i):
                _, hh, qt, g, ng = stream[i]
                s_at = emit_final_b(hh, qt, ng, qpx(i), st_fin.pop(i))
                fin_count[hh] -= 1
                if fin_count[hh] == 0:
                    kb.wait("sync", "S", s_at)
                    dma("sync", f"yat{hh % 2}", ATS[:, hh, :], attnstg[hh % 2][:])
                    wk_free4[hh % 4] = kb.cnt["T"]

        c_yat0 = 16 * uses.get("yat0", 0)
        c_yat1 = 16 * uses.get("yat1", 0)
        stage_barrier()

        attn3 = view3(attn_cat[:], 1024)
        hg3 = view3(hgD[:], 1024)
        kb.wait("sync", "yat0", c_yat0)
        kb.wait("sync", "yat1", c_yat1)
        kb.wait("sync", "ysb", b_ysb)
        yc_n = {}
        for ai in range(8):
            yc_n[ai] = dma("sync", f"yc{ai}", attn3[:, 4 * ai:4 * (ai + 1), :],
                           ATS[:, 4 * ai:4 * (ai + 1), :])
        d_hg_pend = True

        gw_free = {0: 0, 1: 0}
        xd_free = {0: 0, 1: 0, 2: 0, 3: 0}
        dsb_free = {0: None, 1: None, 2: None, 3: None}
        ou_free = {0: 0, 1: 0}
        mm_free = {0: None, 1: None}
        dpend = []
        ob_ctr = [0]

        def flush_dpend(keep):
            while len(dpend) > keep:
                sl4, th, v_d, nxd, dt0, oc0 = dpend.pop(0)
                ob = ob_ctr[0] % 2
                ob_ctr[0] += 1
                kb.wait("tensor", "V", v_d)
                kb.guard(4 + th)
                fns = []
                for i4 in range(4):
                    fns.append(lambda e, o=bank(4 + th)[:, i4 * 128:(i4 + 1) * 128],
                               i=dsb[sl4][:, i4 * 128:(i4 + 1) * 128]:
                               e.transpose(o, i, idf[:]))
                t_tr = t_run(fns)
                dsb_free[sl4] = ("T", t_tr)
                kb.wait("vector", "T", t_tr)
                kb.wait("vector", f"yd{sl4}", nxd)
                kb.wait("vector", f"yo{ob}", ou_free[ob])
                v_o = kb.v(lambda e, o=outbf[ob][:], x=xdb[sl4][:], s=bank(4 + th):
                           e.tensor_tensor(o, s, x, ALU.add))
                xd_free[sl4] = v_o
                kb.release(4 + th, "V", v_o)
                kb.wait("sync", "V", v_o)
                odst = OUT[th * 512:(th + 1) * 512,
                           dt0 * 256 + oc0 * 128: dt0 * 256 + (oc0 + 1) * 128].rearrange(
                    "(i p) j -> p i j", p=128)
                ou_free[ob] = dma("sync", f"yo{ob}", odst, view3(outbf[ob][:], 128))

        un = 0
        for dt in range(16):
            ws = dt % 2
            gw = f"gw{ws}"
            if dt == 0:
                dma("sync", "ya", woutb[0][:, 0:4096], WOUTC[0][:, 0:4096])
                dma("sync", "ya", woutb[0][:, 4096:8192], WOUTC[0][:, 4096:8192])
                ngw = dma("sync", "ya", b2b[0][:], B2C[0])
                dma("sync", "ysb", hg3[:, 0:4, :], HGS[:, 0:4, :])
                d_pre = dma("sync", "ysb", hg3[:, 4:8, :], HGS[:, 4:8, :])
                kb.wait("tensor", "ya", ngw)
            else:
                kb.wait("gpsimd", "T", gw_free[ws])
                dma("gpsimd", gw, woutb[ws][:], WOUTC[dt])
                ngw = dma("gpsimd", gw, b2b[ws][:], B2C[dt])
                kb.wait("tensor", gw, ngw)
            for oc in range(2):
                pp = pair(oc)
                if mm_free[oc]:
                    kb.wait("tensor", mm_free[oc][0], mm_free[oc][1])
                kb.guard(2 * oc, 2 * oc + 1)
                fns = []
                for fc in range(32):
                    if dt == 0 and oc == 0 and fc % 4 == 0:
                        for f2 in fns:
                            kb.raw("tensor", f2)
                        fns = []
                        kb.wait("tensor", f"yc{fc // 4}", yc_n[fc // 4])
                    for th in range(2):
                        fns.append(lambda e, o=pp[:, th * 512:(th + 1) * 512],
                                   l=view3(woutb[ws][:], 256)[:, fc, oc * 128:(oc + 1) * 128],
                                   r=attn3[:, fc, th * 512:(th + 1) * 512],
                                   st=(fc == 0): e.matmul(o, l, r, start=st, stop=False))
                if dt == 0 and oc == 0:
                    kb.wait("tensor", "ysb", d_pre)
                for rc in range(8):
                    for th in range(2):
                        fns.append(lambda e, o=pp[:, th * 512:(th + 1) * 512],
                                   l=view3(b2b[ws][:], 256)[:, rc, oc * 128:(oc + 1) * 128],
                                   r=hg3[:, rc, th * 512:(th + 1) * 512],
                                   sp=(rc == 7): e.matmul(o, l, r, start=False, stop=sp))
                tmm = t_run(fns)
                last_v = None
                for th in range(2):
                    sl4 = un % 4
                    un += 1
                    kb.wait("sync", "V", xd_free[sl4])
                    nxd = dma("sync", f"yd{sl4}", view3(xdb[sl4][:], 128),
                              XOWN[th * 512:(th + 1) * 512,
                                   dt * 256 + oc * 128: dt * 256 + (oc + 1) * 128].rearrange(
                                  "(i p) j -> p i j", p=128))
                    kb.wait("vector", "T", tmm)
                    if dsb_free[sl4]:
                        kb.wait("vector", dsb_free[sl4][0], dsb_free[sl4][1])
                    v_d = kb.v(lambda e, d=dsb[sl4][:], s=pp[:, th * 512:(th + 1) * 512]:
                               e.tensor_copy(d, s))
                    last_v = v_d
                    dpend.append((sl4, th, v_d, nxd, dt, oc))
                mm_free[oc] = ("V", last_v)
                flush_dpend(2)
            gw_free[ws] = kb.cnt["T"]

        flush_dpend(0)
        kb.wait("sync", "yo0", 16 * uses.get("yo0", 0))
        kb.wait("sync", "yo1", 16 * uses.get("yo1", 0))

        with nc.Block() as blk:
            @blk.tensor
            def _(e):
                for f in kb.ops["tensor"]:
                    f(e)

            @blk.vector
            def _(e):
                for f in kb.ops["vector"]:
                    f(e)

            @blk.scalar
            def _(e):
                for f in kb.ops["scalar"]:
                    f(e)

            @blk.gpsimd
            def _(e):
                for f in kb.ops["gpsimd"]:
                    f(e)

            @blk.sync
            def _(e):
                for f in kb.ops["sync"]:
                    f(e)

    return nc


def _host_tables():
    DRh = DR_ // 2
    freqs = (1.0 / (10000.0 ** (np.arange(0, DR_, 2, dtype=np.float32) / np.float32(DR_)))).astype(np.float32)
    wavelengths = (2.0 * np.pi / freqs).astype(np.float32)
    ramp = np.clip((wavelengths / 4096.0 - 1.0) / (32.0 - 1.0), 0.0, 1.0).astype(np.float32)
    scale = (1.0 - ramp + ramp * 40.0).astype(np.float32)
    inv = (freqs / scale).astype(np.float32)
    pos = np.arange(S_, dtype=np.float32)
    f = pos[:, None] * inv[None, :]
    f = np.concatenate([f, f], axis=1).astype(np.float32)
    cos = np.cos(f).astype(np.float32)
    sin = np.sin(f).astype(np.float32)
    sin_signed = sin.copy()
    sin_signed[:, :DRh] = -sin[:, :DRh]
    return cos, sin_signed


def _core_rows(c):
    g = c % 4
    return np.concatenate([np.arange((4 * d + g) * 128, (4 * d + g) * 128 + 128)
                           for d in range(8)])


def kernel(L0_attn_hidden_states, L0_attn_w_q_down, L0_attn_w_q_up_nope,
           L0_attn_w_q_up_rope, L0_attn_w_kv_down, L0_attn_w_k_rope,
           L0_attn_w_uk, L0_attn_w_uv, L0_attn_w_out,
           L0_ffn_A, L0_ffn_B, L0_ffn_Wr):
    x = np.ascontiguousarray(L0_attn_hidden_states, dtype=np.float32)
    wqd0 = np.asarray(L0_attn_w_q_down, dtype=np.float32)
    wqn0 = np.asarray(L0_attn_w_q_up_nope, dtype=np.float32)
    wqr0 = np.asarray(L0_attn_w_q_up_rope, dtype=np.float32)
    wkv0 = np.asarray(L0_attn_w_kv_down, dtype=np.float32)
    wkr0 = np.asarray(L0_attn_w_k_rope, dtype=np.float32)
    wuk = np.ascontiguousarray(np.asarray(L0_attn_w_uk, dtype=np.float32))
    wout0 = np.asarray(L0_attn_w_out, dtype=np.float32)
    A20 = np.asarray(L0_ffn_A, dtype=np.float32).transpose(2, 1, 0).reshape(D_, RK_ * NE_)
    B20 = np.asarray(L0_ffn_B, dtype=np.float32).transpose(2, 0, 1).reshape(RK_ * NE_, D_)
    wrt0 = np.asarray(L0_ffn_Wr, dtype=np.float32).T
    wqd = np.ascontiguousarray(
        wqd0.reshape(32, 128, 12, 128).transpose(2, 1, 0, 3).reshape(12, 128, 4096))
    wqn = np.ascontiguousarray(
        wqn0.reshape(12, 128, 32, 128).transpose(2, 1, 0, 3).reshape(32, 128, 1536))
    wqr = np.ascontiguousarray(
        wqr0.reshape(2, 6, 128, 2048).transpose(0, 2, 1, 3).reshape(2, 128, 12288))
    wkvc = np.ascontiguousarray(
        np.concatenate([wkv0, wkr0], axis=1).reshape(32, 128, 576).transpose(1, 0, 2))
    wuvT = np.ascontiguousarray(
        np.asarray(L0_attn_w_uv, dtype=np.float32).reshape(H_, DV_, KVL_)
        .transpose(0, 2, 1).reshape(H_, 4, 128, 128).transpose(0, 2, 1, 3)
        .reshape(H_, 128, 512))
    wout = np.ascontiguousarray(
        wout0.reshape(32, 128, 16, 256).transpose(2, 1, 0, 3).reshape(16, 128, 8192))
    A2 = np.ascontiguousarray(
        A20.reshape(32, 128, 8, 128).transpose(2, 1, 0, 3).reshape(8, 128, 4096))
    B2 = np.ascontiguousarray(
        B20.reshape(8, 128, 16, 256).transpose(2, 1, 0, 3).reshape(16, 128, 2048))
    wrt = np.ascontiguousarray(
        wrt0.reshape(32, 128, 64).transpose(1, 0, 2).reshape(128, 2048))

    cos, sin_s = _host_tables()
    idb = np.eye(128, dtype=bf16)
    idf = np.eye(128, dtype=np.float32)
    mu = np.ascontiguousarray(np.tril(np.ones((128, 128), dtype=np.float32), -1).astype(bf16))

    nc = build_nc()

    in_maps = []
    for c in range(8):
        b = c // 4
        g = c % 4
        rows = _core_rows(c)
        mv = np.zeros((128, 512), dtype=np.float32)
        jj = np.arange(128)[:, None]
        cc = np.arange(512)[None, :]
        mv[(cc >= jj + g * 128)] = NEG_BIG
        in_maps.append({
            "xkv": np.ascontiguousarray(x[b]),
            "xown": np.ascontiguousarray(x[b][rows]),
            "wqd": wqd, "wqn": wqn, "wqr": wqr, "wkvc": wkvc,
            "wuk": wuk, "wuvT": wuvT, "wout": wout,
            "a2": A2, "b2": B2, "wrt": wrt,
            "c_idb": idb, "c_idf": idf, "c_mu": mu,
            "c_mv": np.ascontiguousarray(mv.astype(bf16)),
            "cosq": np.ascontiguousarray((cos[rows] * SCALE).astype(np.float32)),
            "sinq": np.ascontiguousarray((sin_s[rows] * SCALE).astype(np.float32)),
            "cosk": np.ascontiguousarray(cos),
            "sink": np.ascontiguousarray(sin_s),
        })

    res = run_bass_kernel_spmd(nc, in_maps, core_ids=list(range(8)))

    out = np.empty((B_, S_, D_), dtype=np.float32)
    for c in range(8):
        out[c // 4, _core_rows(c), :] = res.results[c]["out_own"]
    return out
